# version 05
# baseline (speedup 1.0000x reference)
import contextlib
import numpy as np
import concourse.bass as bass
import concourse.mybir as mybir
from concourse.bass_utils import run_bass_kernel_spmd

F32 = mybir.dt.float32
BF16 = mybir.dt.bfloat16
I32 = mybir.dt.int32
F32R = mybir.dt.float32r
AF = mybir.ActivationFunctionType
OP = mybir.AluOpType

NCORES = 8
TOK = 2048
TB = 512
NTILES = TOK // TB
C = 16
D = 2048
HID = 5632
HC = 44
FC = 16
NHP = 16
EPS = 1e-5

ENGINES = ("sync", "gpsimd", "vector", "scalar", "tensor")
SEMS = ("lx", "lg", "sv", "ss", "st", "so", "cc")


class Sched:
    def __init__(self):
        self.blocks = []
        self.cur = None
        self.ctr = {s: 0 for s in SEMS}
        self.block()

    def block(self):
        self.cur = {}
        self.blocks.append(self.cur)

    def emit(self, eng, fn, waits=(), incs=()):
        for s, n in incs:
            self.ctr[s] += n
        self.cur.setdefault(eng, []).append((tuple(waits), fn, tuple(incs)))

    def w(self, *sems):
        return [(s, self.ctr[s]) for s in sems if self.ctr[s] > 0]

    def realize(self, nc, sems):
        for blk in self.blocks:
            if not blk:
                continue
            with nc.Block() as B:
                for eng_name, items in blk.items():
                    def body(eng, items=items):
                        for waits, fn, incs in items:
                            for s, v in waits:
                                if v > 0:
                                    eng.wait_ge(sems[s], v)
                            inst = fn(eng)
                            for s, n in incs:
                                inst.then_inc(sems[s], n)
                    getattr(B, eng_name)(body)


def build_program():
    nc = bass.Bass("TRN2", target_bir_lowering=False, detect_race_conditions=False)

    xt = nc.dram_tensor("xt", [D, TOK], F32, kind="ExternalInput").ap()
    wq_t = nc.dram_tensor("wq_t", [FC, 128, C, 128], F32, kind="ExternalInput").ap()
    wo_t = nc.dram_tensor("wo_t", [FC, 128, C, 128], F32, kind="ExternalInput").ap()
    wk = nc.dram_tensor("wk", [D, D], F32, kind="ExternalInput").ap()
    wv = nc.dram_tensor("wv", [D, D], F32, kind="ExternalInput").ap()
    g_t = nc.dram_tensor("g_t", [HC, 128, C, 128], I32, kind="ExternalInput").ap()
    u_t = nc.dram_tensor("u_t", [HC, 128, C, 128], I32, kind="ExternalInput").ap()
    d_t = nc.dram_tensor("d_t", [FC, 128, HC, 128], I32, kind="ExternalInput").ap()
    bo2 = nc.dram_tensor("bo2", [128, C], F32, kind="ExternalInput").ap()
    sg2 = nc.dram_tensor("sg2", [128, HC], F32, kind="ExternalInput").ap()
    su2 = nc.dram_tensor("su2", [128, HC], F32, kind="ExternalInput").ap()
    sd2 = nc.dram_tensor("sd2", [128, C], F32, kind="ExternalInput").ap()
    sel32_d = nc.dram_tensor("sel32", [32, NHP * 128], F32, kind="ExternalInput").ap()
    out_t = nc.dram_tensor("out_t", [D, TOK], F32, kind="ExternalOutput").ap()
    cc_in = nc.dram_tensor("cc_in", [128, NHP * 65], F32)
    cc_out = nc.dram_tensor("cc_out", [128, NHP * 65], F32)

    xt3 = xt.rearrange("(c p) t -> p c t", p=128)
    out3 = out_t.rearrange("(c p) t -> p c t", p=128)
    wk3 = wk.rearrange("(c p) f -> p c f", p=128)
    wv3 = wv.rearrange("(c p) f -> p c f", p=128)

    S = Sched()

    with contextlib.ExitStack() as ctx:
        E = ctx.enter_context
        ABF = E(nc.sbuf_tensor("ABF", [128, 70848], BF16)).ap()
        AF_ = E(nc.sbuf_tensor("AF", [128, 12960], F32)).ap()
        sel32 = E(nc.sbuf_tensor("sel32_s", [32, NHP * 128], F32)).ap()
        ones128 = E(nc.sbuf_tensor("ones128", [1, 128], F32)).ap()
        rn_f = E(nc.sbuf_tensor("rn_f", [32, TB], F32)).ap()
        bo_s = E(nc.sbuf_tensor("bo_s", [128, C], F32)).ap()
        sg_s = E(nc.sbuf_tensor("sg_s", [128, HC], F32)).ap()
        su_s = E(nc.sbuf_tensor("su_s", [128, HC], F32)).ap()
        sd_s = E(nc.sbuf_tensor("sd_s", [128, C], F32)).ap()
        P = [E(nc.psum_tensor(f"P{i}", [128, 512], F32)).ap() for i in range(8)]
        sems = {s: E(nc.semaphore(s)) for s in SEMS}

        def bfv(off, n):
            return ABF[:, off:off + n]

        kvks_bf = bfv(0, 1040)
        xbs = [bfv(1056, 512), bfv(1568, 512)]
        sqb = [bfv(2080, 512), bfv(2592, 512)]
        g_tmp = [bfv(3104, 512), bfv(3616, 512)]
        h2s_row = ABF[0:1, 4128:4640]
        gus_row = ABF[0:1, 4640:5152]
        negc_row = ABF[0:1, 5152:5280]
        onescol_bf = ABF[:, 5280:5281]
        RB = 5312
        h1a = bfv(RB, C * TOK).rearrange("p (c t) -> p c t", t=TOK)
        kfb = bfv(RB + 32768, C * 512).rearrange("p (g f) -> p g f", f=512)
        vfb = bfv(RB + 40960, C * 512).rearrange("p (g f) -> p g f", f=512)
        wkb = bfv(RB + 49152, C * 512).rearrange("p (c f) -> p c f", f=512)
        wvb = bfv(RB + 57344, C * 512).rearrange("p (c f) -> p c f", f=512)
        hbuf = bfv(RB, C * TB).rearrange("p (c t) -> p c t", t=TB)
        qo = bfv(RB + 8192, C * TB).rearrange("p (c t) -> p c t", t=TB)
        aiy = bfv(RB + 16384, C * TB).rearrange("p (c t) -> p c t", t=TB)
        gu = bfv(RB + 24576, HC * TB).rearrange("p (c t) -> p c t", t=TB)
        w1 = bfv(RB + 47104, C * 128).rearrange("p (c f) -> p c f", f=128)
        w2 = bfv(RB + 49152, C * 128).rearrange("p (c f) -> p c f", f=128)
        w3 = [bfv(RB + 51200, C * 128).rearrange("p (c f) -> p c f", f=128),
              bfv(RB + 53248, C * 128).rearrange("p (c f) -> p c f", f=128)]
        w4 = [bfv(RB + 55296, C * 128).rearrange("p (c f) -> p c f", f=128),
              bfv(RB + 57344, C * 128).rearrange("p (c f) -> p c f", f=128)]
        dw = [bfv(RB, HC * 128).rearrange("p (c f) -> p c f", f=128),
              bfv(RB + 5632, HC * 128).rearrange("p (c f) -> p c f", f=128)]

        def fv(off, n):
            return AF_[:, off:off + n]

        kvks_sb = fv(0, 1040)
        rm = AF_[0:1, 1040:1552]
        rr = AF_[0:1, 1552:2064]
        rowA = AF_[0:1, 2064:2576]
        rowB = AF_[0:1, 2576:3088]
        rb_sb = [fv(3088, 512), fv(3600, 512)]
        rn_tok = fv(4112, 128)
        tsc = fv(4240, 512)
        XB = fv(4768, C * 512).rearrange("p (c t) -> p c t", t=512)

        S.emit("sync", lambda e: e.dma_start(out=sel32, in_=sel32_d), incs=[("lx", 16)])
        S.emit("sync", lambda e: e.dma_start(out=bo_s, in_=bo2), incs=[("lx", 16)])
        S.emit("sync", lambda e: e.dma_start(out=sg_s, in_=sg2), incs=[("lx", 16)])
        S.emit("sync", lambda e: e.dma_start(out=su_s, in_=su2), incs=[("lx", 16)])
        S.emit("sync", lambda e: e.dma_start(out=sd_s, in_=sd2), incs=[("lx", 16)])
        S.emit("vector", lambda e: e.memset(ones128, 1.0), incs=[("sv", 1)])
        S.emit("vector", lambda e: e.memset(negc_row, -128.0), incs=[("sv", 1)])
        S.emit("vector", lambda e: e.memset(onescol_bf, 1.0), incs=[("sv", 1)])

        def ln_stats(src3, src_is_f32, T=TB):
            onec = onescol_bf
            st_b = S.ctr["st"]
            if src_is_f32:
                sv_b, ss_b = S.ctr["sv"], S.ctr["ss"]
                for c in range(C):
                    scr_w = [("st", st_b + c - 1)] if c >= 2 else []
                    S.emit("vector",
                           lambda e, c=c: e.tensor_copy(xbs[c % 2][:, :T], src3[:, c, :]),
                           waits=scr_w, incs=[("sv", 1)])
                    S.emit("scalar",
                           lambda e, c=c: e.activation(sqb[c % 2][:, :T], src3[:, c, :], AF.Square),
                           waits=scr_w, incs=[("ss", 1)])
                    S.emit("tensor",
                           lambda e, c=c: e.matmul(P[5][0:1, :T], onec, xbs[c % 2][:, :T],
                                                   start=(c == 0), stop=(c == C - 1)),
                           waits=[("sv", sv_b + c + 1), ("ss", ss_b + c + 1)])
                    S.emit("tensor",
                           lambda e, c=c: e.matmul(P[6][0:1, :T], onec, sqb[c % 2][:, :T],
                                                   start=(c == 0), stop=(c == C - 1)),
                           incs=[("st", 1)])
            else:
                ss_b = S.ctr["ss"]
                for c in range(C):
                    scr_w = [("st", st_b + c - 1)] if c >= 2 else []
                    S.emit("scalar",
                           lambda e, c=c: e.activation(sqb[c % 2][:, :T], src3[:, c, :], AF.Square),
                           waits=scr_w, incs=[("ss", 1)])
                    S.emit("tensor",
                           lambda e, c=c: e.matmul(P[5][0:1, :T], onec, src3[:, c, :],
                                                   start=(c == 0), stop=(c == C - 1)))
                    S.emit("tensor",
                           lambda e, c=c: e.matmul(P[6][0:1, :T], onec, sqb[c % 2][:, :T],
                                                   start=(c == 0), stop=(c == C - 1)),
                           waits=[("ss", ss_b + c + 1)], incs=[("st", 1)])
            wt = S.w("st")
            S.emit("vector", lambda e: e.tensor_scalar(rm[:, :T], P[5][0:1, :T], 1.0 / D, None, OP.mult),
                   waits=wt)
            S.emit("vector", lambda e: e.tensor_scalar(rowA[:, :T], P[6][0:1, :T], 1.0 / D, EPS,
                                                       OP.mult, OP.add))
            S.emit("vector", lambda e: e.tensor_tensor(rowB[:, :T], rm[:, :T], rm[:, :T], OP.mult))
            S.emit("vector", lambda e: e.tensor_tensor(rowA[:, :T], rowA[:, :T], rowB[:, :T], OP.subtract),
                   incs=[("sv", 1)])
            S.emit("scalar", lambda e: e.activation(rowB[:, :T], rowA[:, :T], AF.Sqrt),
                   waits=S.w("sv"), incs=[("ss", 1)])
            S.emit("vector", lambda e: e.reciprocal(rr[:, :T], rowB[:, :T]),
                   waits=S.w("ss"), incs=[("sv", 1)])
            wv_ = S.w("sv")
            S.emit("tensor", lambda e: e.matmul(P[2][:, :T], ones128, rm[:, :T],
                                                start=True, stop=True), waits=wv_)
            S.emit("tensor", lambda e: e.matmul(P[3][:, :T], ones128, rr[:, :T],
                                                start=True, stop=True), incs=[("st", 1)])

        def ln_apply(src3, dst3, T=TB):
            wt = S.w("st")
            for c in range(C):
                S.emit("vector", lambda e, c=c: e.tensor_tensor(tsc[:, :T], src3[:, c, :], P[2][:, :T], OP.subtract),
                       waits=wt if c == 0 else [])
                S.emit("vector", lambda e, c=c: e.tensor_tensor(dst3[:, c, :], tsc[:, :T], P[3][:, :T], OP.mult),
                       incs=[("sv", 1)])

        def ln_residual(src3, T=TB):
            wt = S.w("st")
            for c in range(C):
                S.emit("vector", lambda e, c=c: e.tensor_tensor(tsc[:, :T], src3[:, c, :], P[2][:, :T], OP.subtract),
                       waits=wt if c == 0 else [])
                S.emit("vector", lambda e, c=c: e.tensor_tensor(tsc[:, :T], tsc[:, :T], P[3][:, :T], OP.mult))
                S.emit("vector", lambda e, c=c: e.tensor_tensor(XB[:, c, :T], XB[:, c, :T], tsc[:, :T], OP.add),
                       incs=[("sv", 1)])

        S.block()
        for t in range(NTILES):
            t0 = t * TB
            S.emit("sync", lambda e, t0=t0: e.dma_start(out=XB[:, :, :], in_=xt3[:, :, t0:t0 + TB]),
                   waits=S.w("sv"), incs=[("lx", 16)])
            wx = S.w("lx")
            S.emit("vector", lambda e: e.nop(), waits=wx)
            S.emit("scalar", lambda e: e.nop(), waits=wx)
            ln_stats(XB, True)
            ln_apply(XB, h1a[:, :, t0:t0 + TB])

        S.block()
        for fb in range(4):
            wfree = S.w("st")
            S.emit("gpsimd", lambda e, fb=fb: e.dma_start(out=wkb, in_=wk3[:, :, fb * 512:(fb + 1) * 512]),
                   waits=wfree, incs=[("lg", 16)])
            S.emit("gpsimd", lambda e, fb=fb: e.dma_start(out=wvb, in_=wv3[:, :, fb * 512:(fb + 1) * 512]),
                   incs=[("lg", 16)])
            wl = S.w("lg")
            ss_b, sv_b = S.ctr["ss"], S.ctr["sv"]
            for g in range(C):
                pk = P[0] if g % 2 == 0 else P[1]
                pv = P[2] if g % 2 == 0 else P[3]
                if g == 0:
                    first_w = wl
                elif g >= 2:
                    first_w = [("ss", ss_b + g - 1), ("sv", sv_b + g - 1)]
                else:
                    first_w = []
                for c in range(C):
                    S.emit("tensor", lambda e, c=c, g=g, pk=pk: e.matmul(
                        pk[:, :], h1a[:, c, fbg(fb, g)], wkb[:, c, :],
                        start=(c == 0), stop=(c == C - 1)),
                        waits=(first_w if c == 0 else []),
                        incs=([("st", 1)] if c == C - 1 else []))
                wt_k = S.w("st")
                for c in range(C):
                    S.emit("tensor", lambda e, c=c, g=g, pv=pv: e.matmul(
                        pv[:, :], h1a[:, c, fbg(fb, g)], wvb[:, c, :],
                        start=(c == 0), stop=(c == C - 1)),
                        incs=([("st", 1)] if c == C - 1 else []))
                wt_v = S.w("st")
                S.emit("scalar", lambda e, g=g, pk=pk: e.activation(kfb[:, g, :], pk[:, :], AF.Relu),
                       waits=wt_k, incs=[("ss", 1)])
                S.emit("vector", lambda e, g=g, pv=pv: e.tensor_copy(vfb[:, g, :], pv[:, :]),
                       waits=wt_v, incs=[("sv", 1)])
            ev_all = [("ss", S.ctr["ss"]), ("sv", S.ctr["sv"])]
            for hp in range(4):
                for g in range(C):
                    S.emit("tensor", lambda e, g=g, hp=hp: e.matmul(
                        P[4][:, hp * 128:(hp + 1) * 128],
                        kfb[:, g, hp * 128:(hp + 1) * 128],
                        vfb[:, g, hp * 128:(hp + 1) * 128],
                        start=(g == 0), stop=(g == C - 1)),
                        waits=(ev_all if (hp == 0 and g == 0) else []))
                    S.emit("tensor", lambda e, g=g, hp=hp: e.matmul(
                        P[5][:, hp:hp + 1],
                        kfb[:, g, hp * 128:(hp + 1) * 128], onescol_bf,
                        start=(g == 0), stop=(g == C - 1)),
                        incs=([("st", 1)] if (hp == 3 and g == C - 1) else []))
            wt = S.w("st")
            for hp in range(4):
                col = (fb * 4 + hp) * 65
                S.emit("vector", lambda e, hp=hp, col=col: e.tensor_copy(
                    kvks_sb[0:64, col:col + 64], P[4][0:64, hp * 128:hp * 128 + 64]),
                    waits=(wt if hp == 0 else []))
                S.emit("vector", lambda e, hp=hp, col=col: e.tensor_copy(
                    kvks_sb[64:128, col:col + 64], P[4][64:128, hp * 128 + 64:hp * 128 + 128]))
                S.emit("vector", lambda e, hp=hp, col=col: e.tensor_copy(
                    kvks_sb[:, col + 64:col + 65], P[5][:, hp:hp + 1]),
                    incs=[("sv", 1)])
            S.emit("tensor", lambda e: e.nop(), waits=S.w("sv"))

        S.block()
        S.emit("gpsimd", lambda e: e.dma_start(out=cc_in.ap(), in_=kvks_sb), waits=S.w("sv"),
               incs=[("lg", 16)])
        S.emit("gpsimd", lambda e: e.collective_compute(
            "AllReduce", OP.add, replica_groups=[list(range(NCORES))],
            ins=[cc_in.ap().opt()], outs=[cc_out.ap().opt()]),
            waits=S.w("lg"), incs=[("cc", 1)])
        S.emit("gpsimd", lambda e: e.dma_start(out=kvks_sb, in_=cc_out.ap()), waits=S.w("cc"),
               incs=[("lg", 16)])
        S.emit("vector", lambda e: e.tensor_copy(kvks_bf, kvks_sb), waits=S.w("lg"),
               incs=[("sv", 1)])

        for t in range(NTILES):
            t0 = t * TB
            S.block()
            S.emit("sync", lambda e, t0=t0: e.dma_start(out=XB[:, :, :], in_=xt3[:, :, t0:t0 + TB]),
                   waits=[("so", 16 * t)], incs=[("lx", 16)])
            wx = S.w("lx")
            S.emit("vector", lambda e: e.nop(), waits=wx)
            S.emit("scalar", lambda e: e.nop(), waits=wx)

            def wload(dst, src_ap, free_val):
                S.emit("gpsimd", lambda e, dst=dst, src_ap=src_ap: e.dma_start(out=dst, in_=src_ap),
                       waits=[("st", free_val)] if free_val > 0 else [],
                       incs=[("lg", 16)])

            tile_st0 = S.ctr["st"]
            ln_stats(XB, True)
            ln_apply(XB, hbuf)
            h1_done = S.w("sv")

            wq_free = []
            ss_b = S.ctr["ss"]
            for fc in range(FC):
                wload(w1 if fc % 2 == 0 else w2, wq_t[fc], wq_free[fc - 2] if fc >= 2 else tile_st0)
                nl = S.ctr["lg"]
                pq = P[0] if fc % 2 == 0 else P[1]
                wb = w1 if fc % 2 == 0 else w2
                first_w = [("lg", nl)]
                if fc == 0:
                    first_w += h1_done
                if fc >= 2:
                    first_w += [("ss", ss_b + fc - 1)]
                for c in range(C):
                    S.emit("tensor", lambda e, c=c, pq=pq, wb=wb: e.matmul(
                        pq[:, :], wb[:, c, :], hbuf[:, c, :],
                        start=(c == 0), stop=(c == C - 1)),
                        waits=(first_w if c == 0 else []),
                        incs=([("st", 1)] if c == C - 1 else []))
                wq_free.append(S.ctr["st"])
                S.emit("scalar", lambda e, fc=fc, pq=pq: e.activation(qo[:, fc, :], pq[:, :], AF.Relu),
                       waits=S.w("st"), incs=[("ss", 1)])

            wq_done = S.w("ss")
            for tc in range(4):
                for hp in range(NHP):
                    col0 = tc * 32 + 2 * hp
                    S.emit("tensor", lambda e, tc=tc, hp=hp, col0=col0: e.matmul(
                        P[4][:, col0:col0 + 1],
                        qo[0:64, hp, tc * 128:(tc + 1) * 128],
                        kvks_bf[0:64, hp * 65 + 64:hp * 65 + 65],
                        start=True, stop=True),
                        waits=(wq_done if (tc == 0 and hp == 0) else []))
                    S.emit("tensor", lambda e, tc=tc, hp=hp, col0=col0: e.matmul(
                        P[4][:, col0 + 1:col0 + 2],
                        qo[64:128, hp, tc * 128:(tc + 1) * 128],
                        kvks_bf[64:128, hp * 65 + 64:hp * 65 + 65],
                        start=True, stop=True, tile_position=(64, 0)),
                        incs=([("st", 1)] if (tc == 3 and hp == NHP - 1) else []))
            S.emit("vector", lambda e: e.tensor_scalar(rn_tok, P[4][:, 0:128], 1e-6, None, OP.max),
                   waits=S.w("st"))
            S.emit("vector", lambda e: e.reciprocal(rn_tok, rn_tok))
            for tc in range(4):
                for g in range(4):
                    S.emit("vector", lambda e, tc=tc, g=g: e.transpose(
                        rn_f[0:32, tc * 128 + g * 32:tc * 128 + (g + 1) * 32],
                        rn_tok[g * 32:(g + 1) * 32, tc * 32:(tc + 1) * 32]),
                        incs=([("sv", 1)] if (tc == 3 and g == 3) else []))

            rn_ready = S.w("sv")
            sv_b, ss_b = S.ctr["sv"], S.ctr["ss"]
            for hp in range(NHP):
                pq = P[0] if hp % 2 == 0 else P[1]
                pr = P[2] if hp % 2 == 0 else P[3]
                first_w = list(rn_ready) if hp == 0 else []
                if hp >= 2:
                    first_w += [("sv", sv_b + hp - 1)]
                S.emit("tensor", lambda e, hp=hp, pq=pq: e.matmul(
                    pq[0:64, :], kvks_bf[0:64, hp * 65:hp * 65 + 64], qo[0:64, hp, :],
                    start=True, stop=True), waits=first_w)
                S.emit("tensor", lambda e, hp=hp, pq=pq: e.matmul(
                    pq[64:128, :], kvks_bf[64:128, hp * 65:hp * 65 + 64], qo[64:128, hp, :],
                    start=True, stop=True, tile_position=(64, 64)))
                S.emit("tensor", lambda e, hp=hp, pr=pr: e.matmul(
                    pr[:, :], sel32[:, hp * 128:(hp + 1) * 128], rn_f,
                    start=True, stop=True),
                    incs=[("st", 1)])
                sw = [("sv", sv_b + hp - 1)] if hp >= 2 else []
                S.emit("scalar", lambda e, hp=hp, pr=pr: e.activation(rb_sb[hp % 2], pr[:, :], AF.Identity),
                       waits=S.w("st") + sw, incs=[("ss", 1)])
                we = S.w("ss")
                S.emit("vector", lambda e, hp=hp, pq=pq: e.tensor_tensor(
                    aiy[0:64, hp, :], pq[0:64, :], rb_sb[hp % 2][0:64, :], OP.mult), waits=we)
                S.emit("vector", lambda e, hp=hp, pq=pq: e.tensor_tensor(
                    aiy[64:128, hp, :], pq[64:128, :], rb_sb[hp % 2][64:128, :], OP.mult),
                    incs=[("sv", 1)])

            ai_done = S.w("sv")
            wo_free = []
            ss_b = S.ctr["ss"]
            for fc in range(FC):
                wload(w1 if fc % 2 == 0 else w2, wo_t[fc], wq_free[-1] if fc < 2 else wo_free[fc - 2])
                nl = S.ctr["lg"]
                po = P[0] if fc % 2 == 0 else P[1]
                wb = w1 if fc % 2 == 0 else w2
                first_w = [("lg", nl)]
                if fc == 0:
                    first_w += ai_done
                if fc >= 2:
                    first_w += [("ss", ss_b + fc - 1)]
                for c in range(C):
                    S.emit("tensor", lambda e, c=c, po=po, wb=wb: e.matmul(
                        po[:, :], wb[:, c, :], aiy[:, c, :],
                        start=(c == 0), stop=(c == C - 1)),
                        waits=(first_w if c == 0 else []),
                        incs=([("st", 1)] if c == C - 1 else []))
                wo_free.append(S.ctr["st"])
                S.emit("scalar", lambda e, fc=fc, po=po: e.activation(
                    qo[:, fc, :], po[:, :], AF.Identity, bias=bo_s[:, fc:fc + 1]),
                    waits=S.w("st"), incs=[("ss", 1)])

            S.emit("tensor", lambda e: e.nop(), waits=S.w("ss"))
            ln_stats(qo, False)
            ln_residual(qo)

            ln_stats(XB, True)
            ln_apply(XB, hbuf)

            h2_done = S.w("sv")
            for c in range(C):
                S.emit("tensor", lambda e, c=c: e.matmul(
                    P[5][0:1, :], onescol_bf, hbuf[:, c, :],
                    start=(c == 0), stop=(c == C - 1)),
                    waits=(h2_done if c == 0 else []),
                    incs=([("st", 1)] if c == C - 1 else []))
            S.emit("vector", lambda e: e.tensor_copy(h2s_row, P[5][0:1, :]), waits=S.w("st"),
                   incs=[("sv", 1)])

            h2s_ready = S.w("sv")
            gfree = []
            ss_b, sv_b = S.ctr["ss"], S.ctr["sv"]
            for hc in range(HC):
                wload(w3[hc % 2], g_t[hc], gfree[hc - 2] if hc >= 2 else tile_st0)
                wload(w4[hc % 2], u_t[hc], 0)
                nl = S.ctr["lg"]
                pg = P[0] if hc % 2 == 0 else P[2]
                pu = P[1] if hc % 2 == 0 else P[3]
                first_w = [("lg", nl)]
                if hc == 0:
                    first_w += h2s_ready
                if hc >= 2:
                    first_w += [("ss", ss_b + hc - 1), ("sv", sv_b + hc - 1)]
                if hc >= 2:
                    S.emit("tensor", lambda e, hc=hc: e.matmul(
                        P[6][0:1, :], onescol_bf, gu[:, hc - 2, :],
                        start=(hc - 2 == 0), stop=False),
                        waits=[("sv", sv_b + hc - 1)])
                for c in range(C):
                    S.emit("tensor", lambda e, c=c, hc=hc, pg=pg: e.matmul(
                        pg[:, :], w3[hc % 2][:, c, :], hbuf[:, c, :],
                        start=(c == 0), stop=False),
                        waits=(first_w if c == 0 else []))
                S.emit("tensor", lambda e, pg=pg: e.matmul(
                    pg[:, :], negc_row, h2s_row, start=False, stop=True))
                for c in range(C):
                    S.emit("tensor", lambda e, c=c, hc=hc, pu=pu: e.matmul(
                        pu[:, :], w4[hc % 2][:, c, :], hbuf[:, c, :],
                        start=(c == 0), stop=False))
                S.emit("tensor", lambda e, pu=pu: e.matmul(
                    pu[:, :], negc_row, h2s_row, start=False, stop=True),
                    incs=[("st", 1)])
                gfree.append(S.ctr["st"])
                wt = S.w("st")
                S.emit("scalar", lambda e, hc=hc, pg=pg: e.activation(
                    g_tmp[hc % 2], pg[:, :], AF.Silu, scale=sg_s[:, hc:hc + 1]),
                    waits=wt, incs=[("ss", 1)])
                S.emit("vector", lambda e, hc=hc, pu=pu: e.scalar_tensor_tensor(
                    gu[:, hc, :], pu[:, :], su_s[:, hc:hc + 1], g_tmp[hc % 2],
                    OP.mult, OP.mult),
                    waits=S.w("ss"), incs=[("sv", 1)])
            for hc in (HC - 2, HC - 1):
                S.emit("tensor", lambda e, hc=hc: e.matmul(
                    P[6][0:1, :], onescol_bf, gu[:, hc, :],
                    start=False, stop=(hc == HC - 1)),
                    waits=[("sv", sv_b + hc + 1)],
                    incs=([("st", 1)] if hc == HC - 1 else []))
            S.emit("vector", lambda e: e.tensor_copy(gus_row, P[6][0:1, :]), waits=S.w("st"),
                   incs=[("sv", 1)])

            gus_ready = S.w("sv")
            dfree = []
            ss_b = S.ctr["ss"]
            for oc in range(FC):
                wload(dw[oc % 2], d_t[oc], dfree[oc - 2] if oc >= 2 else gfree[-1])
                nl = S.ctr["lg"]
                pd = P[0] if oc % 2 == 0 else P[1]
                first_w = [("lg", nl)]
                if oc == 0:
                    first_w += gus_ready
                if oc >= 2:
                    first_w += [("ss", ss_b + oc - 1)]
                for c in range(HC):
                    S.emit("tensor", lambda e, c=c, oc=oc, pd=pd: e.matmul(
                        pd[:, :], dw[oc % 2][:, c, :], gu[:, c, :],
                        start=(c == 0), stop=False),
                        waits=(first_w if c == 0 else []))
                S.emit("tensor", lambda e, pd=pd: e.matmul(
                    pd[:, :], negc_row, gus_row, start=False, stop=True),
                    incs=[("st", 1)])
                dfree.append(S.ctr["st"])
                S.emit("scalar", lambda e, oc=oc, pd=pd: e.activation(
                    aiy[:, oc, :], pd[:, :], AF.Identity, scale=sd_s[:, oc:oc + 1]),
                    waits=S.w("st"), incs=[("ss", 1)])

            S.emit("tensor", lambda e: e.nop(), waits=S.w("ss"))
            ln_stats(aiy, False)
            ln_residual(aiy)

            S.emit("sync", lambda e, t0=t0: e.dma_start(out=out3[:, :, t0:t0 + TB], in_=XB[:, :, :]),
                   waits=S.w("sv"), incs=[("so", 16)])

        S.block()
        S.emit("sync", lambda e: e.nop(), waits=[("so", 16 * NTILES)])

        S.realize(nc, sems)
    return nc


def fbg(fb, g):
    return slice(g * 128, (g + 1) * 128)


_NC_CACHE = None


def _get_program():
    global _NC_CACHE
    if _NC_CACHE is None:
        _NC_CACHE = build_program()
    return _NC_CACHE


def kernel(L0_attn_hidden_states, L0_attn_Wq, L0_attn_Wk, L0_attn_Wv, L0_attn_Wo,
           L0_attn_bo, L0_ffn_wq_gate, L0_ffn_s_gate, L0_ffn_wq_up, L0_ffn_s_up,
           L0_ffn_wq_down, L0_ffn_s_down):
    x = np.asarray(L0_attn_hidden_states)
    S_FULL = x.shape[1]
    xT = np.ascontiguousarray(x[0].T)

    def tile_proj(W):
        return np.ascontiguousarray(
            np.asarray(W, dtype=np.float32).reshape(C, 128, FC, 128).transpose(2, 1, 0, 3))

    wq_t = tile_proj(L0_attn_Wq)
    wo_t = tile_proj(L0_attn_Wo)
    wk = np.ascontiguousarray(np.asarray(L0_attn_Wk, dtype=np.float32))
    wv = np.ascontiguousarray(np.asarray(L0_attn_Wv, dtype=np.float32))

    def tile_quant(Wq_int, out_chunks):
        W = np.asarray(Wq_int, dtype=np.int32)
        O, I = W.shape
        WT = np.ascontiguousarray(W.T)
        return np.ascontiguousarray(
            WT.reshape(I // 128, 128, out_chunks, 128).transpose(2, 1, 0, 3))

    g_t = tile_quant(L0_ffn_wq_gate, HC)
    u_t = tile_quant(L0_ffn_wq_up, HC)
    d_t = tile_quant(L0_ffn_wq_down, FC)

    def col2(v, nch):
        return np.ascontiguousarray(np.asarray(v, dtype=np.float32).reshape(nch, 128).T)

    bo2 = col2(L0_attn_bo, C)
    sg2 = col2(L0_ffn_s_gate, HC)
    su2 = col2(L0_ffn_s_up, HC)
    sd2 = col2(L0_ffn_s_down, C)

    sel32 = np.zeros((32, NHP * 128), dtype=np.float32)
    for hp in range(NHP):
        sel32[2 * hp, hp * 128:hp * 128 + 64] = 1.0
        sel32[2 * hp + 1, hp * 128 + 64:(hp + 1) * 128] = 1.0

    nc = _get_program()
    shared = {"wq_t": wq_t, "wo_t": wo_t, "wk": wk, "wv": wv, "g_t": g_t,
              "u_t": u_t, "d_t": d_t, "bo2": bo2, "sg2": sg2, "su2": su2,
              "sd2": sd2, "sel32": sel32}
    in_maps = []
    for c in range(NCORES):
        m = dict(shared)
        m["xt"] = np.ascontiguousarray(xT[:, c * TOK:(c + 1) * TOK])
        in_maps.append(m)

    res = run_bass_kernel_spmd(nc, in_maps, core_ids=list(range(NCORES)))
    out = np.empty((D, S_FULL), dtype=np.float32)
    for c in range(NCORES):
        out[:, c * TOK:(c + 1) * TOK] = res.results[c]["out_t"]
    return np.ascontiguousarray(out.T).reshape(1, S_FULL, D)


if __name__ == "__main__":
    print("building program...")
    nc = build_program()
    n = sum(1 for _ in nc.all_instructions())
    print("instructions:", n)


# version 06
# speedup vs baseline: 1.0193x; 1.0193x over previous
import contextlib
import numpy as np
import concourse.bass as bass
import concourse.mybir as mybir
from concourse.bass_utils import run_bass_kernel_spmd

F32 = mybir.dt.float32
BF16 = mybir.dt.bfloat16
I32 = mybir.dt.int32
F32R = mybir.dt.float32r
AF = mybir.ActivationFunctionType
OP = mybir.AluOpType

NCORES = 8
TOK = 2048
TB = 512
NTILES = TOK // TB
C = 16
D = 2048
HID = 5632
HC = 44
FC = 16
NHP = 16
EPS = 1e-5

ENGINES = ("sync", "gpsimd", "vector", "scalar", "tensor")
SEMS = ("lx", "lg", "sv", "ss", "st", "so", "cc", "sc")


class Sched:
    def __init__(self):
        self.blocks = []
        self.cur = None
        self.ctr = {s: 0 for s in SEMS}
        self.block()

    def block(self):
        self.cur = {}
        self.blocks.append(self.cur)

    def emit(self, eng, fn, waits=(), incs=()):
        for s, n in incs:
            self.ctr[s] += n
        self.cur.setdefault(eng, []).append((tuple(waits), fn, tuple(incs)))

    def w(self, *sems):
        return [(s, self.ctr[s]) for s in sems if self.ctr[s] > 0]

    def realize(self, nc, sems):
        for blk in self.blocks:
            if not blk:
                continue
            with nc.Block() as B:
                for eng_name, items in blk.items():
                    def body(eng, items=items):
                        for waits, fn, incs in items:
                            for s, v in waits:
                                if v > 0:
                                    eng.wait_ge(sems[s], v)
                            inst = fn(eng)
                            for s, n in incs:
                                inst.then_inc(sems[s], n)
                    getattr(B, eng_name)(body)


def build_program():
    nc = bass.Bass("TRN2", target_bir_lowering=False, detect_race_conditions=False)

    xt = nc.dram_tensor("xt", [D, TOK], F32, kind="ExternalInput").ap()
    wq_t = nc.dram_tensor("wq_t", [FC, 128, C, 128], F32, kind="ExternalInput").ap()
    wo_t = nc.dram_tensor("wo_t", [FC, 128, C, 128], F32, kind="ExternalInput").ap()
    wk = nc.dram_tensor("wk", [D, D], F32, kind="ExternalInput").ap()
    wv = nc.dram_tensor("wv", [D, D], F32, kind="ExternalInput").ap()
    g_t = nc.dram_tensor("g_t", [HC, 128, C, 128], I32, kind="ExternalInput").ap()
    u_t = nc.dram_tensor("u_t", [HC, 128, C, 128], I32, kind="ExternalInput").ap()
    d_t = nc.dram_tensor("d_t", [FC, 128, HC, 128], I32, kind="ExternalInput").ap()
    bo2 = nc.dram_tensor("bo2", [128, C], F32, kind="ExternalInput").ap()
    sg2 = nc.dram_tensor("sg2", [128, HC], F32, kind="ExternalInput").ap()
    su2 = nc.dram_tensor("su2", [128, HC], F32, kind="ExternalInput").ap()
    sd2 = nc.dram_tensor("sd2", [128, C], F32, kind="ExternalInput").ap()
    sel32_d = nc.dram_tensor("sel32", [32, NHP * 128], F32, kind="ExternalInput").ap()
    out_t = nc.dram_tensor("out_t", [D, TOK], F32, kind="ExternalOutput").ap()
    cc_in = nc.dram_tensor("cc_in", [128, NHP * 65], F32)
    cc_out = nc.dram_tensor("cc_out", [128, NHP * 65], F32)
    gcache = nc.dram_tensor("gcache", [HC, 128, C * 128], BF16).ap()
    ucache = nc.dram_tensor("ucache", [HC, 128, C * 128], BF16).ap()
    dcache = nc.dram_tensor("dcache", [FC, 128, HC * 128], BF16).ap()
    acache = nc.dram_tensor("acache", [2 * FC, 128, C * 128], BF16).ap()

    xt3 = xt.rearrange("(c p) t -> p c t", p=128)
    out3 = out_t.rearrange("(c p) t -> p c t", p=128)
    wk3 = wk.rearrange("(c p) f -> p c f", p=128)
    wv3 = wv.rearrange("(c p) f -> p c f", p=128)

    S = Sched()

    with contextlib.ExitStack() as ctx:
        E = ctx.enter_context
        ABF = E(nc.sbuf_tensor("ABF", [128, 70848], BF16)).ap()
        AF_ = E(nc.sbuf_tensor("AF", [128, 12960], F32)).ap()
        sel32 = E(nc.sbuf_tensor("sel32_s", [32, NHP * 128], F32)).ap()
        ones128 = E(nc.sbuf_tensor("ones128", [1, 128], F32)).ap()
        rn_f = E(nc.sbuf_tensor("rn_f", [32, TB], F32)).ap()
        bo_s = E(nc.sbuf_tensor("bo_s", [128, C], F32)).ap()
        sg_s = E(nc.sbuf_tensor("sg_s", [128, HC], F32)).ap()
        su_s = E(nc.sbuf_tensor("su_s", [128, HC], F32)).ap()
        sd_s = E(nc.sbuf_tensor("sd_s", [128, C], F32)).ap()
        P = [E(nc.psum_tensor(f"P{i}", [128, 512], F32)).ap() for i in range(8)]
        sems = {s: E(nc.semaphore(s)) for s in SEMS}

        def bfv(off, n):
            return ABF[:, off:off + n]

        kvks_bf = bfv(0, 1040)
        xbs = [bfv(1056, 512), bfv(1568, 512)]
        sqb = [bfv(2080, 512), bfv(2592, 512)]
        g_tmp = [bfv(3104, 512), bfv(3616, 512)]
        h2s_row = ABF[0:1, 4128:4640]
        gus_row = ABF[0:1, 4640:5152]
        negc_row = ABF[0:1, 5152:5280]
        onescol_bf = ABF[:, 5280:5281]
        RB = 5312
        h1a = bfv(RB, C * TOK).rearrange("p (c t) -> p c t", t=TOK)
        kfb = bfv(RB + 32768, C * 512).rearrange("p (g f) -> p g f", f=512)
        vfb = bfv(RB + 40960, C * 512).rearrange("p (g f) -> p g f", f=512)
        wkb = bfv(RB + 49152, C * 512).rearrange("p (c f) -> p c f", f=512)
        wvb = bfv(RB + 57344, C * 512).rearrange("p (c f) -> p c f", f=512)
        hbuf = bfv(RB, C * TB).rearrange("p (c t) -> p c t", t=TB)
        qo = bfv(RB + 8192, C * TB).rearrange("p (c t) -> p c t", t=TB)
        aiy = bfv(RB + 16384, C * TB).rearrange("p (c t) -> p c t", t=TB)
        gu = bfv(RB + 24576, HC * TB).rearrange("p (c t) -> p c t", t=TB)
        w1 = bfv(RB + 47104, C * 128).rearrange("p (c f) -> p c f", f=128)
        w2 = bfv(RB + 49152, C * 128).rearrange("p (c f) -> p c f", f=128)
        w3 = [bfv(RB + 51200, C * 128).rearrange("p (c f) -> p c f", f=128),
              bfv(RB + 53248, C * 128).rearrange("p (c f) -> p c f", f=128)]
        w4 = [bfv(RB + 55296, C * 128).rearrange("p (c f) -> p c f", f=128),
              bfv(RB + 57344, C * 128).rearrange("p (c f) -> p c f", f=128)]
        dw = [bfv(RB + 8192, HC * 128).rearrange("p (c f) -> p c f", f=128),
              bfv(RB, HC * 128).rearrange("p (c f) -> p c f", f=128)]

        def fv(off, n):
            return AF_[:, off:off + n]

        kvks_sb = fv(0, 1040)
        rm = AF_[0:1, 1040:1552]
        rr = AF_[0:1, 1552:2064]
        rowA = AF_[0:1, 2064:2576]
        rowB = AF_[0:1, 2576:3088]
        rb_sb = [fv(3088, 512), fv(3600, 512)]
        rn_tok = fv(4112, 128)
        tsc = fv(4240, 512)
        XB = fv(4768, C * 512).rearrange("p (c t) -> p c t", t=512)

        S.emit("sync", lambda e: e.dma_start(out=sel32, in_=sel32_d), incs=[("lx", 16)])
        S.emit("sync", lambda e: e.dma_start(out=bo_s, in_=bo2), incs=[("lx", 16)])
        S.emit("sync", lambda e: e.dma_start(out=sg_s, in_=sg2), incs=[("lx", 16)])
        S.emit("sync", lambda e: e.dma_start(out=su_s, in_=su2), incs=[("lx", 16)])
        S.emit("sync", lambda e: e.dma_start(out=sd_s, in_=sd2), incs=[("lx", 16)])
        S.emit("vector", lambda e: e.memset(ones128, 1.0), incs=[("sv", 1)])
        S.emit("vector", lambda e: e.memset(negc_row, -128.0), incs=[("sv", 1)])
        S.emit("vector", lambda e: e.memset(onescol_bf, 1.0), incs=[("sv", 1)])

        def ln_stats(src3, src_is_f32, T=TB):
            onec = onescol_bf
            st_b = S.ctr["st"]
            if src_is_f32:
                sv_b, ss_b = S.ctr["sv"], S.ctr["ss"]
                for c in range(C):
                    scr_w = [("st", st_b + c - 1)] if c >= 2 else []
                    S.emit("vector",
                           lambda e, c=c: e.tensor_copy(xbs[c % 2][:, :T], src3[:, c, :]),
                           waits=scr_w, incs=[("sv", 1)])
                    S.emit("scalar",
                           lambda e, c=c: e.activation(sqb[c % 2][:, :T], src3[:, c, :], AF.Square),
                           waits=scr_w, incs=[("ss", 1)])
                    S.emit("tensor",
                           lambda e, c=c: e.matmul(P[5][0:1, :T], onec, xbs[c % 2][:, :T],
                                                   start=(c == 0), stop=(c == C - 1)),
                           waits=[("sv", sv_b + c + 1), ("ss", ss_b + c + 1)])
                    S.emit("tensor",
                           lambda e, c=c: e.matmul(P[6][0:1, :T], onec, sqb[c % 2][:, :T],
                                                   start=(c == 0), stop=(c == C - 1)),
                           incs=[("st", 1)])
            else:
                ss_b = S.ctr["ss"]
                for c in range(C):
                    scr_w = [("st", st_b + c - 1)] if c >= 2 else []
                    S.emit("scalar",
                           lambda e, c=c: e.activation(sqb[c % 2][:, :T], src3[:, c, :], AF.Square),
                           waits=scr_w, incs=[("ss", 1)])
                    S.emit("tensor",
                           lambda e, c=c: e.matmul(P[5][0:1, :T], onec, src3[:, c, :],
                                                   start=(c == 0), stop=(c == C - 1)))
                    S.emit("tensor",
                           lambda e, c=c: e.matmul(P[6][0:1, :T], onec, sqb[c % 2][:, :T],
                                                   start=(c == 0), stop=(c == C - 1)),
                           waits=[("ss", ss_b + c + 1)], incs=[("st", 1)])
            wt = S.w("st")
            S.emit("vector", lambda e: e.tensor_scalar(rm[:, :T], P[5][0:1, :T], 1.0 / D, None, OP.mult),
                   waits=wt)
            S.emit("vector", lambda e: e.tensor_scalar(rowA[:, :T], P[6][0:1, :T], 1.0 / D, EPS,
                                                       OP.mult, OP.add))
            S.emit("vector", lambda e: e.tensor_tensor(rowB[:, :T], rm[:, :T], rm[:, :T], OP.mult))
            S.emit("vector", lambda e: e.tensor_tensor(rowA[:, :T], rowA[:, :T], rowB[:, :T], OP.subtract),
                   incs=[("sv", 1)])
            S.emit("scalar", lambda e: e.activation(rowB[:, :T], rowA[:, :T], AF.Sqrt),
                   waits=S.w("sv"), incs=[("ss", 1)])
            S.emit("vector", lambda e: e.reciprocal(rr[:, :T], rowB[:, :T]),
                   waits=S.w("ss"), incs=[("sv", 1)])
            wv_ = S.w("sv")
            S.emit("tensor", lambda e: e.matmul(P[2][:, :T], ones128, rm[:, :T],
                                                start=True, stop=True), waits=wv_)
            S.emit("tensor", lambda e: e.matmul(P[3][:, :T], ones128, rr[:, :T],
                                                start=True, stop=True), incs=[("st", 1)])

        def ln_apply(src3, dst3, T=TB):
            wt = S.w("st")
            for c in range(C):
                S.emit("vector", lambda e, c=c: e.tensor_tensor(tsc[:, :T], src3[:, c, :], P[2][:, :T], OP.subtract),
                       waits=wt if c == 0 else [])
                S.emit("vector", lambda e, c=c: e.tensor_tensor(dst3[:, c, :], tsc[:, :T], P[3][:, :T], OP.mult),
                       incs=[("sv", 1)])

        def ln_residual(src3, T=TB):
            wt = S.w("st")
            for c in range(C):
                S.emit("vector", lambda e, c=c: e.tensor_tensor(tsc[:, :T], src3[:, c, :], P[2][:, :T], OP.subtract),
                       waits=wt if c == 0 else [])
                S.emit("vector", lambda e, c=c: e.tensor_tensor(tsc[:, :T], tsc[:, :T], P[3][:, :T], OP.mult))
                S.emit("vector", lambda e, c=c: e.tensor_tensor(XB[:, c, :T], XB[:, c, :T], tsc[:, :T], OP.add),
                       incs=[("sv", 1)])

        S.block()
        for t in range(NTILES):
            t0 = t * TB
            S.emit("sync", lambda e, t0=t0: e.dma_start(out=XB[:, :, :], in_=xt3[:, :, t0:t0 + TB]),
                   waits=S.w("sv"), incs=[("lx", 16)])
            wx = S.w("lx")
            S.emit("vector", lambda e: e.nop(), waits=wx)
            S.emit("scalar", lambda e: e.nop(), waits=wx)
            ln_stats(XB, True)
            ln_apply(XB, h1a[:, :, t0:t0 + TB])

        S.block()
        for fb in range(4):
            wfree = S.w("st")
            S.emit("gpsimd", lambda e, fb=fb: e.dma_start(out=wkb, in_=wk3[:, :, fb * 512:(fb + 1) * 512]),
                   waits=wfree, incs=[("lg", 16)])
            S.emit("gpsimd", lambda e, fb=fb: e.dma_start(out=wvb, in_=wv3[:, :, fb * 512:(fb + 1) * 512]),
                   incs=[("lg", 16)])
            wl = S.w("lg")
            ss_b, sv_b = S.ctr["ss"], S.ctr["sv"]
            for g in range(C):
                pk = P[0] if g % 2 == 0 else P[1]
                pv = P[2] if g % 2 == 0 else P[3]
                if g == 0:
                    first_w = wl
                elif g >= 2:
                    first_w = [("ss", ss_b + g - 1), ("sv", sv_b + g - 1)]
                else:
                    first_w = []
                for c in range(C):
                    S.emit("tensor", lambda e, c=c, g=g, pk=pk: e.matmul(
                        pk[:, :], h1a[:, c, fbg(fb, g)], wkb[:, c, :],
                        start=(c == 0), stop=(c == C - 1)),
                        waits=(first_w if c == 0 else []),
                        incs=([("st", 1)] if c == C - 1 else []))
                wt_k = S.w("st")
                for c in range(C):
                    S.emit("tensor", lambda e, c=c, g=g, pv=pv: e.matmul(
                        pv[:, :], h1a[:, c, fbg(fb, g)], wvb[:, c, :],
                        start=(c == 0), stop=(c == C - 1)),
                        incs=([("st", 1)] if c == C - 1 else []))
                wt_v = S.w("st")
                S.emit("scalar", lambda e, g=g, pk=pk: e.activation(kfb[:, g, :], pk[:, :], AF.Relu),
                       waits=wt_k, incs=[("ss", 1)])
                S.emit("vector", lambda e, g=g, pv=pv: e.tensor_copy(vfb[:, g, :], pv[:, :]),
                       waits=wt_v, incs=[("sv", 1)])
            ev_all = [("ss", S.ctr["ss"]), ("sv", S.ctr["sv"])]
            for hp in range(4):
                for g in range(C):
                    S.emit("tensor", lambda e, g=g, hp=hp: e.matmul(
                        P[4][:, hp * 128:(hp + 1) * 128],
                        kfb[:, g, hp * 128:(hp + 1) * 128],
                        vfb[:, g, hp * 128:(hp + 1) * 128],
                        start=(g == 0), stop=(g == C - 1)),
                        waits=(ev_all if (hp == 0 and g == 0) else []))
                    S.emit("tensor", lambda e, g=g, hp=hp: e.matmul(
                        P[5][:, hp:hp + 1],
                        kfb[:, g, hp * 128:(hp + 1) * 128], onescol_bf,
                        start=(g == 0), stop=(g == C - 1)),
                        incs=([("st", 1)] if (hp == 3 and g == C - 1) else []))
            wt = S.w("st")
            for hp in range(4):
                col = (fb * 4 + hp) * 65
                S.emit("vector", lambda e, hp=hp, col=col: e.tensor_copy(
                    kvks_sb[0:64, col:col + 64], P[4][0:64, hp * 128:hp * 128 + 64]),
                    waits=(wt if hp == 0 else []))
                S.emit("vector", lambda e, hp=hp, col=col: e.tensor_copy(
                    kvks_sb[64:128, col:col + 64], P[4][64:128, hp * 128 + 64:hp * 128 + 128]))
                S.emit("vector", lambda e, hp=hp, col=col: e.tensor_copy(
                    kvks_sb[:, col + 64:col + 65], P[5][:, hp:hp + 1]),
                    incs=[("sv", 1)])
            S.emit("tensor", lambda e: e.nop(), waits=S.w("sv"))

        S.block()
        S.emit("gpsimd", lambda e: e.dma_start(out=cc_in.ap(), in_=kvks_sb), waits=S.w("sv"),
               incs=[("lg", 16)])
        S.emit("gpsimd", lambda e: e.collective_compute(
            "AllReduce", OP.add, replica_groups=[list(range(NCORES))],
            ins=[cc_in.ap().opt()], outs=[cc_out.ap().opt()]),
            waits=S.w("lg"), incs=[("cc", 1)])
        S.emit("gpsimd", lambda e: e.dma_start(out=kvks_sb, in_=cc_out.ap()), waits=S.w("cc"),
               incs=[("lg", 16)])
        S.emit("vector", lambda e: e.tensor_copy(kvks_bf, kvks_sb), waits=S.w("lg"),
               incs=[("sv", 1)])

        for t in range(NTILES):
            t0 = t * TB
            S.block()
            S.emit("sync", lambda e, t0=t0: e.dma_start(out=XB[:, :, :], in_=xt3[:, :, t0:t0 + TB]),
                   waits=[("so", 16 * t)], incs=[("lx", 16)])
            wx = S.w("lx")
            S.emit("vector", lambda e: e.nop(), waits=wx)
            S.emit("scalar", lambda e: e.nop(), waits=wx)

            def wload(dst, raw_src, cache_src, waits):
                if t == 0:
                    S.emit("gpsimd", lambda e, dst=dst, s=raw_src: e.dma_start(out=dst, in_=s),
                           waits=waits, incs=[("lg", 16)])
                    dep = ("lg", S.ctr["lg"])
                    S.emit("sync", lambda e, dst=dst, s=cache_src: e.dma_start(out=s, in_=dst),
                           waits=[dep], incs=[("sc", 16)])
                    return dep, S.ctr["sc"]
                else:
                    S.emit("sync", lambda e, dst=dst, s=cache_src: e.dma_start(out=dst, in_=s),
                           waits=waits, incs=[("lx", 16)])
                    return ("lx", S.ctr["lx"]), 0

            tile_st0 = S.ctr["st"]
            sc_full = S.ctr["sc"]
            ln_stats(XB, True)
            ln_apply(XB, hbuf)
            h1_done = S.w("sv")

            wq_free = []
            wq_sc = []
            ss_b = S.ctr["ss"]
            for fc in range(FC):
                fw = [("st", wq_free[fc - 2])] if fc >= 2 else [("st", tile_st0)]
                if t == 0 and fc >= 2:
                    fw.append(("sc", wq_sc[fc - 2]))
                if t == 1 and fc == 0:
                    fw.append(("sc", sc_full))
                wb = w1 if fc % 2 == 0 else w2
                dep, csc = wload(wb, wq_t[fc], acache[fc], fw)
                wq_sc.append(csc)
                pq = P[0] if fc % 2 == 0 else P[1]
                first_w = [dep]
                if fc == 0:
                    first_w += h1_done
                if fc >= 2:
                    first_w += [("ss", ss_b + fc - 1)]
                for c in range(C):
                    S.emit("tensor", lambda e, c=c, pq=pq, wb=wb: e.matmul(
                        pq[:, :], wb[:, c, :], hbuf[:, c, :],
                        start=(c == 0), stop=(c == C - 1)),
                        waits=(first_w if c == 0 else []),
                        incs=([("st", 1)] if c == C - 1 else []))
                wq_free.append(S.ctr["st"])
                S.emit("scalar", lambda e, fc=fc, pq=pq: e.activation(qo[:, fc, :], pq[:, :], AF.Relu),
                       waits=S.w("st"), incs=[("ss", 1)])

            wq_done = S.w("ss")
            for tc in range(4):
                for hp in range(NHP):
                    col0 = tc * 32 + 2 * hp
                    S.emit("tensor", lambda e, tc=tc, hp=hp, col0=col0: e.matmul(
                        P[4][:, col0:col0 + 1],
                        qo[0:64, hp, tc * 128:(tc + 1) * 128],
                        kvks_bf[0:64, hp * 65 + 64:hp * 65 + 65],
                        start=True, stop=True),
                        waits=(wq_done if (tc == 0 and hp == 0) else []))
                    S.emit("tensor", lambda e, tc=tc, hp=hp, col0=col0: e.matmul(
                        P[4][:, col0 + 1:col0 + 2],
                        qo[64:128, hp, tc * 128:(tc + 1) * 128],
                        kvks_bf[64:128, hp * 65 + 64:hp * 65 + 65],
                        start=True, stop=True, tile_position=(64, 0)),
                        incs=([("st", 1)] if (tc == 3 and hp == NHP - 1) else []))
            S.emit("vector", lambda e: e.tensor_scalar(rn_tok, P[4][:, 0:128], 1e-6, None, OP.max),
                   waits=S.w("st"))
            S.emit("vector", lambda e: e.reciprocal(rn_tok, rn_tok))
            for tc in range(4):
                for g in range(4):
                    S.emit("vector", lambda e, tc=tc, g=g: e.transpose(
                        rn_f[0:32, tc * 128 + g * 32:tc * 128 + (g + 1) * 32],
                        rn_tok[g * 32:(g + 1) * 32, tc * 32:(tc + 1) * 32]),
                        incs=([("sv", 1)] if (tc == 3 and g == 3) else []))

            rn_ready = S.w("sv")
            sv_b, ss_b = S.ctr["sv"], S.ctr["ss"]
            for hp in range(NHP):
                pq = P[0] if hp % 2 == 0 else P[1]
                pr = P[2] if hp % 2 == 0 else P[3]
                first_w = list(rn_ready) if hp == 0 else []
                if hp >= 2:
                    first_w += [("sv", sv_b + hp - 1)]
                S.emit("tensor", lambda e, hp=hp, pq=pq: e.matmul(
                    pq[0:64, :], kvks_bf[0:64, hp * 65:hp * 65 + 64], qo[0:64, hp, :],
                    start=True, stop=True), waits=first_w)
                S.emit("tensor", lambda e, hp=hp, pq=pq: e.matmul(
                    pq[64:128, :], kvks_bf[64:128, hp * 65:hp * 65 + 64], qo[64:128, hp, :],
                    start=True, stop=True, tile_position=(64, 64)))
                S.emit("tensor", lambda e, hp=hp, pr=pr: e.matmul(
                    pr[:, :], sel32[:, hp * 128:(hp + 1) * 128], rn_f,
                    start=True, stop=True),
                    incs=[("st", 1)])
                sw = [("sv", sv_b + hp - 1)] if hp >= 2 else []
                S.emit("scalar", lambda e, hp=hp, pr=pr: e.activation(rb_sb[hp % 2], pr[:, :], AF.Identity),
                       waits=S.w("st") + sw, incs=[("ss", 1)])
                we = S.w("ss")
                S.emit("vector", lambda e, hp=hp, pq=pq: e.tensor_tensor(
                    aiy[0:64, hp, :], pq[0:64, :], rb_sb[hp % 2][0:64, :], OP.mult), waits=we)
                S.emit("vector", lambda e, hp=hp, pq=pq: e.tensor_tensor(
                    aiy[64:128, hp, :], pq[64:128, :], rb_sb[hp % 2][64:128, :], OP.mult),
                    incs=[("sv", 1)])

            ai_done = S.w("sv")
            wo_free = []
            wo_sc = []
            ss_b = S.ctr["ss"]
            for fc in range(FC):
                fw = [("st", wq_free[-1])] if fc < 2 else [("st", wo_free[fc - 2])]
                if t == 0:
                    fw.append(("sc", wq_sc[-1] if fc < 2 else wo_sc[fc - 2]))
                wb = w1 if fc % 2 == 0 else w2
                dep, csc = wload(wb, wo_t[fc], acache[FC + fc], fw)
                wo_sc.append(csc)
                po = P[0] if fc % 2 == 0 else P[1]
                first_w = [dep]
                if fc == 0:
                    first_w += ai_done
                if fc >= 2:
                    first_w += [("ss", ss_b + fc - 1)]
                for c in range(C):
                    S.emit("tensor", lambda e, c=c, po=po, wb=wb: e.matmul(
                        po[:, :], wb[:, c, :], aiy[:, c, :],
                        start=(c == 0), stop=(c == C - 1)),
                        waits=(first_w if c == 0 else []),
                        incs=([("st", 1)] if c == C - 1 else []))
                wo_free.append(S.ctr["st"])
                S.emit("scalar", lambda e, fc=fc, po=po: e.activation(
                    qo[:, fc, :], po[:, :], AF.Identity, bias=bo_s[:, fc:fc + 1]),
                    waits=S.w("st"), incs=[("ss", 1)])

            S.emit("tensor", lambda e: e.nop(), waits=S.w("ss"))
            ln_stats(qo, False)
            ln_residual(qo)
            resid1_done = S.ctr["sv"]

            ln_stats(XB, True)
            ln_apply(XB, hbuf)

            h2_done = S.w("sv")
            for c in range(C):
                S.emit("tensor", lambda e, c=c: e.matmul(
                    P[5][0:1, :], onescol_bf, hbuf[:, c, :],
                    start=(c == 0), stop=(c == C - 1)),
                    waits=(h2_done if c == 0 else []),
                    incs=([("st", 1)] if c == C - 1 else []))
            S.emit("vector", lambda e: e.tensor_copy(h2s_row, P[5][0:1, :]), waits=S.w("st"),
                   incs=[("sv", 1)])

            h2s_ready = S.w("sv")
            gfree = []
            gu_sc = []
            ss_b, sv_b = S.ctr["ss"], S.ctr["sv"]
            for hc in range(HC):
                fw = [("st", gfree[hc - 2])] if hc >= 2 else [("st", tile_st0)]
                if t == 0 and hc >= 2:
                    fw.append(("sc", gu_sc[hc - 2]))
                wload(w3[hc % 2], g_t[hc], gcache[hc], fw)
                dep, _ = wload(w4[hc % 2], u_t[hc], ucache[hc], [])
                gu_sc.append(S.ctr["sc"])
                if hc == 7:
                    dw0_dep, dw0_csc = wload(dw[0], d_t[0], dcache[0],
                                             [("sv", resid1_done)])
                pg = P[0] if hc % 2 == 0 else P[2]
                pu = P[1] if hc % 2 == 0 else P[3]
                first_w = [dep]
                if hc == 0:
                    first_w += h2s_ready
                if hc >= 2:
                    first_w += [("ss", ss_b + hc - 1), ("sv", sv_b + hc - 1)]
                if hc >= 2:
                    S.emit("tensor", lambda e, hc=hc: e.matmul(
                        P[6][0:1, :], onescol_bf, gu[:, hc - 2, :],
                        start=(hc - 2 == 0), stop=False),
                        waits=[("sv", sv_b + hc - 1)])
                for c in range(C):
                    S.emit("tensor", lambda e, c=c, hc=hc, pg=pg: e.matmul(
                        pg[:, :], w3[hc % 2][:, c, :], hbuf[:, c, :],
                        start=(c == 0), stop=False),
                        waits=(first_w if c == 0 else []))
                S.emit("tensor", lambda e, pg=pg: e.matmul(
                    pg[:, :], negc_row, h2s_row, start=False, stop=True))
                for c in range(C):
                    S.emit("tensor", lambda e, c=c, hc=hc, pu=pu: e.matmul(
                        pu[:, :], w4[hc % 2][:, c, :], hbuf[:, c, :],
                        start=(c == 0), stop=False))
                S.emit("tensor", lambda e, pu=pu: e.matmul(
                    pu[:, :], negc_row, h2s_row, start=False, stop=True),
                    incs=[("st", 1)])
                gfree.append(S.ctr["st"])
                wt = S.w("st")
                S.emit("scalar", lambda e, hc=hc, pg=pg: e.activation(
                    g_tmp[hc % 2], pg[:, :], AF.Silu, scale=sg_s[:, hc:hc + 1]),
                    waits=wt, incs=[("ss", 1)])
                S.emit("vector", lambda e, hc=hc, pu=pu: e.scalar_tensor_tensor(
                    gu[:, hc, :], pu[:, :], su_s[:, hc:hc + 1], g_tmp[hc % 2],
                    OP.mult, OP.mult),
                    waits=S.w("ss"), incs=[("sv", 1)])
            for hc in (HC - 2, HC - 1):
                S.emit("tensor", lambda e, hc=hc: e.matmul(
                    P[6][0:1, :], onescol_bf, gu[:, hc, :],
                    start=False, stop=(hc == HC - 1)),
                    waits=[("sv", sv_b + hc + 1)],
                    incs=([("st", 1)] if hc == HC - 1 else []))
            S.emit("vector", lambda e: e.tensor_copy(gus_row, P[6][0:1, :]), waits=S.w("st"),
                   incs=[("sv", 1)])

            gus_ready = S.w("sv")
            dfree = []
            dn_sc = []
            ss_b = S.ctr["ss"]
            for oc in range(FC):
                if oc == 0:
                    dep, csc = dw0_dep, dw0_csc
                else:
                    if oc == 1:
                        fw = [("st", gfree[-1])]
                    else:
                        fw = [("st", dfree[oc - 2])]
                        if t == 0:
                            fw.append(("sc", dn_sc[oc - 2]))
                    dep, csc = wload(dw[oc % 2], d_t[oc], dcache[oc], fw)
                dn_sc.append(csc)
                pd = P[0] if oc % 2 == 0 else P[1]
                first_w = [dep]
                if oc == 0:
                    first_w += gus_ready
                if oc >= 2:
                    first_w += [("ss", ss_b + oc - 1)]
                for c in range(HC):
                    S.emit("tensor", lambda e, c=c, oc=oc, pd=pd: e.matmul(
                        pd[:, :], dw[oc % 2][:, c, :], gu[:, c, :],
                        start=(c == 0), stop=False),
                        waits=(first_w if c == 0 else []))
                S.emit("tensor", lambda e, pd=pd: e.matmul(
                    pd[:, :], negc_row, gus_row, start=False, stop=True),
                    incs=[("st", 1)])
                dfree.append(S.ctr["st"])
                S.emit("scalar", lambda e, oc=oc, pd=pd: e.activation(
                    aiy[:, oc, :], pd[:, :], AF.Identity, scale=sd_s[:, oc:oc + 1]),
                    waits=S.w("st"), incs=[("ss", 1)])

            S.emit("tensor", lambda e: e.nop(), waits=S.w("ss"))
            ln_stats(aiy, False)
            ln_residual(aiy)

            S.emit("sync", lambda e, t0=t0: e.dma_start(out=out3[:, :, t0:t0 + TB], in_=XB[:, :, :]),
                   waits=S.w("sv"), incs=[("so", 16)])

        S.block()
        S.emit("sync", lambda e: e.nop(), waits=[("so", 16 * NTILES)])

        S.realize(nc, sems)
    return nc


def fbg(fb, g):
    return slice(g * 128, (g + 1) * 128)


_NC_CACHE = None


def _get_program():
    global _NC_CACHE
    if _NC_CACHE is None:
        _NC_CACHE = build_program()
    return _NC_CACHE


def kernel(L0_attn_hidden_states, L0_attn_Wq, L0_attn_Wk, L0_attn_Wv, L0_attn_Wo,
           L0_attn_bo, L0_ffn_wq_gate, L0_ffn_s_gate, L0_ffn_wq_up, L0_ffn_s_up,
           L0_ffn_wq_down, L0_ffn_s_down):
    x = np.asarray(L0_attn_hidden_states)
    S_FULL = x.shape[1]
    xT = np.ascontiguousarray(x[0].T)

    def tile_proj(W):
        return np.ascontiguousarray(
            np.asarray(W, dtype=np.float32).reshape(C, 128, FC, 128).transpose(2, 1, 0, 3))

    wq_t = tile_proj(L0_attn_Wq)
    wo_t = tile_proj(L0_attn_Wo)
    wk = np.ascontiguousarray(np.asarray(L0_attn_Wk, dtype=np.float32))
    wv = np.ascontiguousarray(np.asarray(L0_attn_Wv, dtype=np.float32))

    def tile_quant(Wq_int, out_chunks):
        W = np.asarray(Wq_int, dtype=np.int32)
        O, I = W.shape
        WT = np.ascontiguousarray(W.T)
        return np.ascontiguousarray(
            WT.reshape(I // 128, 128, out_chunks, 128).transpose(2, 1, 0, 3))

    g_t = tile_quant(L0_ffn_wq_gate, HC)
    u_t = tile_quant(L0_ffn_wq_up, HC)
    d_t = tile_quant(L0_ffn_wq_down, FC)

    def col2(v, nch):
        return np.ascontiguousarray(np.asarray(v, dtype=np.float32).reshape(nch, 128).T)

    bo2 = col2(L0_attn_bo, C)
    sg2 = col2(L0_ffn_s_gate, HC)
    su2 = col2(L0_ffn_s_up, HC)
    sd2 = col2(L0_ffn_s_down, C)

    sel32 = np.zeros((32, NHP * 128), dtype=np.float32)
    for hp in range(NHP):
        sel32[2 * hp, hp * 128:hp * 128 + 64] = 1.0
        sel32[2 * hp + 1, hp * 128 + 64:(hp + 1) * 128] = 1.0

    nc = _get_program()
    shared = {"wq_t": wq_t, "wo_t": wo_t, "wk": wk, "wv": wv, "g_t": g_t,
              "u_t": u_t, "d_t": d_t, "bo2": bo2, "sg2": sg2, "su2": su2,
              "sd2": sd2, "sel32": sel32}
    in_maps = []
    for c in range(NCORES):
        m = dict(shared)
        m["xt"] = np.ascontiguousarray(xT[:, c * TOK:(c + 1) * TOK])
        in_maps.append(m)

    res = run_bass_kernel_spmd(nc, in_maps, core_ids=list(range(NCORES)))
    out = np.empty((D, S_FULL), dtype=np.float32)
    for c in range(NCORES):
        out[:, c * TOK:(c + 1) * TOK] = res.results[c]["out_t"]
    return np.ascontiguousarray(out.T).reshape(1, S_FULL, D)


if __name__ == "__main__":
    print("building program...")
    nc = build_program()
    n = sum(1 for _ in nc.all_instructions())
    print("instructions:", n)


# version 10
# speedup vs baseline: 1.0198x; 1.0005x over previous
import contextlib
import numpy as np
import concourse.bass as bass
import concourse.mybir as mybir
from concourse.bass_utils import run_bass_kernel_spmd

F32 = mybir.dt.float32
BF16 = mybir.dt.bfloat16
I32 = mybir.dt.int32
F32R = mybir.dt.float32r
AF = mybir.ActivationFunctionType
OP = mybir.AluOpType

NCORES = 8
TOK = 2048
TB = 512
NTILES = TOK // TB
C = 16
D = 2048
HID = 5632
HC = 44
FC = 16
NHP = 16
EPS = 1e-5

ENGINES = ("sync", "gpsimd", "vector", "scalar", "tensor")
SEMS = ("lx", "lg", "sv", "ss", "st", "so", "cc", "sc", "sg")
NSPLIT = 16


class Sched:
    def __init__(self):
        self.blocks = []
        self.cur = None
        self.ctr = {s: 0 for s in SEMS}
        self.block()

    def block(self):
        self.cur = {}
        self.blocks.append(self.cur)

    def emit(self, eng, fn, waits=(), incs=()):
        for s, n in incs:
            self.ctr[s] += n
        self.cur.setdefault(eng, []).append((tuple(waits), fn, tuple(incs)))

    def w(self, *sems):
        return [(s, self.ctr[s]) for s in sems if self.ctr[s] > 0]

    def realize(self, nc, sems):
        for blk in self.blocks:
            if not blk:
                continue
            with nc.Block() as B:
                for eng_name, items in blk.items():
                    def body(eng, items=items):
                        for waits, fn, incs in items:
                            for s, v in waits:
                                if v > 0:
                                    eng.wait_ge(sems[s], v)
                            inst = fn(eng)
                            for s, n in incs:
                                inst.then_inc(sems[s], n)
                    getattr(B, eng_name)(body)


def build_program():
    nc = bass.Bass("TRN2", target_bir_lowering=False, detect_race_conditions=False)

    xt = nc.dram_tensor("xt", [D, TOK], F32, kind="ExternalInput").ap()
    wq_t = nc.dram_tensor("wq_t", [FC, 128, C, 128], F32, kind="ExternalInput").ap()
    wo_t = nc.dram_tensor("wo_t", [FC, 128, C, 128], F32, kind="ExternalInput").ap()
    wk = nc.dram_tensor("wk", [D, D], F32, kind="ExternalInput").ap()
    wv = nc.dram_tensor("wv", [D, D], F32, kind="ExternalInput").ap()
    g_t = nc.dram_tensor("g_t", [HC, 128, C, 128], I32, kind="ExternalInput").ap()
    u_t = nc.dram_tensor("u_t", [HC, 128, C, 128], I32, kind="ExternalInput").ap()
    d_t = nc.dram_tensor("d_t", [FC, 128, HC, 128], I32, kind="ExternalInput").ap()
    bo2 = nc.dram_tensor("bo2", [128, C], F32, kind="ExternalInput").ap()
    sg2 = nc.dram_tensor("sg2", [128, HC], F32, kind="ExternalInput").ap()
    su2 = nc.dram_tensor("su2", [128, HC], F32, kind="ExternalInput").ap()
    sd2 = nc.dram_tensor("sd2", [128, C], F32, kind="ExternalInput").ap()
    sel32_d = nc.dram_tensor("sel32", [32, NHP * 128], F32, kind="ExternalInput").ap()
    out_t = nc.dram_tensor("out_t", [D, TOK], F32, kind="ExternalOutput").ap()
    cc_in = nc.dram_tensor("cc_in", [128, NHP * 65], F32)
    cc_out = nc.dram_tensor("cc_out", [128, NHP * 65], F32)
    gcache = nc.dram_tensor("gcache", [HC, 128, C * 128], BF16).ap()
    ucache = nc.dram_tensor("ucache", [HC, 128, C * 128], BF16).ap()
    dcache = nc.dram_tensor("dcache", [FC, 128, HC * 128], BF16).ap()
    acache = nc.dram_tensor("acache", [2 * FC, 128, C * 128], BF16).ap()

    xt3 = xt.rearrange("(c p) t -> p c t", p=128)
    out3 = out_t.rearrange("(c p) t -> p c t", p=128)
    wk3 = wk.rearrange("(c p) f -> p c f", p=128)
    wv3 = wv.rearrange("(c p) f -> p c f", p=128)

    S = Sched()

    with contextlib.ExitStack() as ctx:
        E = ctx.enter_context
        ABF = E(nc.sbuf_tensor("ABF", [128, 70848], BF16)).ap()
        AF_ = E(nc.sbuf_tensor("AF", [128, 13472], F32)).ap()
        sel32 = E(nc.sbuf_tensor("sel32_s", [32, NHP * 128], F32)).ap()
        ones128 = E(nc.sbuf_tensor("ones128", [1, 128], F32)).ap()
        rn_f = E(nc.sbuf_tensor("rn_f", [32, TB], F32)).ap()
        bo_s = E(nc.sbuf_tensor("bo_s", [128, C], F32)).ap()
        sg_s = E(nc.sbuf_tensor("sg_s", [128, HC], F32)).ap()
        su_s = E(nc.sbuf_tensor("su_s", [128, HC], F32)).ap()
        sd_s = E(nc.sbuf_tensor("sd_s", [128, C], F32)).ap()
        P = [E(nc.psum_tensor(f"P{i}", [128, 512], F32)).ap() for i in range(8)]
        sems = {s: E(nc.semaphore(s)) for s in SEMS}

        def bfv(off, n):
            return ABF[:, off:off + n]

        kvks_bf = bfv(0, 1040)
        xbs = [bfv(1056, 512), bfv(1568, 512)]
        sqb = [bfv(2080, 512), bfv(2592, 512)]
        g_tmp = [bfv(3104, 512), bfv(3616, 512)]
        h2s_row = ABF[0:1, 4128:4640]
        gus_row = ABF[0:1, 4640:5152]
        negc_row = ABF[0:1, 5152:5280]
        onescol_bf = ABF[:, 5280:5281]
        RB = 5312
        h1a = bfv(RB, C * TOK).rearrange("p (c t) -> p c t", t=TOK)
        kfb = bfv(RB + 32768, C * 512).rearrange("p (g f) -> p g f", f=512)
        vfb = bfv(RB + 40960, C * 512).rearrange("p (g f) -> p g f", f=512)
        wkb = bfv(RB + 49152, C * 512).rearrange("p (c f) -> p c f", f=512)
        wvb = bfv(RB + 57344, C * 512).rearrange("p (c f) -> p c f", f=512)
        hbuf = bfv(RB, C * TB).rearrange("p (c t) -> p c t", t=TB)
        qo = bfv(RB + 8192, C * TB).rearrange("p (c t) -> p c t", t=TB)
        aiy = bfv(RB + 16384, C * TB).rearrange("p (c t) -> p c t", t=TB)
        gu = bfv(RB + 24576, HC * TB).rearrange("p (c t) -> p c t", t=TB)
        w1 = bfv(RB + 47104, C * 128).rearrange("p (c f) -> p c f", f=128)
        w2 = bfv(RB + 49152, C * 128).rearrange("p (c f) -> p c f", f=128)
        w3 = [bfv(RB + 51200, C * 128).rearrange("p (c f) -> p c f", f=128),
              bfv(RB + 53248, C * 128).rearrange("p (c f) -> p c f", f=128)]
        w4 = [bfv(RB + 55296, C * 128).rearrange("p (c f) -> p c f", f=128),
              bfv(RB + 57344, C * 128).rearrange("p (c f) -> p c f", f=128)]
        dw = [bfv(RB + 8192, HC * 128).rearrange("p (c f) -> p c f", f=128),
              bfv(RB, HC * 128).rearrange("p (c f) -> p c f", f=128)]

        def fv(off, n):
            return AF_[:, off:off + n]

        kvks_sb = fv(0, 1040)
        rm = AF_[0:1, 1040:1552]
        rr = AF_[0:1, 1552:2064]
        rowA = AF_[0:1, 2064:2576]
        rowB = AF_[0:1, 2576:3088]
        rb_sb = [fv(3088, 512), fv(3600, 512)]
        rn_tok = fv(4112, 128)
        tsc = fv(4240, 512)
        XB = fv(4768, C * 512).rearrange("p (c t) -> p c t", t=512)
        tsc2 = fv(12960, 512)
        mb_sb = kvks_sb[:, 0:512]
        rbx_sb = kvks_sb[:, 512:1024]

        S.emit("sync", lambda e: e.dma_start(out=sel32, in_=sel32_d), incs=[("lx", 16)])
        S.emit("sync", lambda e: e.dma_start(out=bo_s, in_=bo2), incs=[("lx", 16)])
        S.emit("sync", lambda e: e.dma_start(out=sg_s, in_=sg2), incs=[("lx", 16)])
        S.emit("sync", lambda e: e.dma_start(out=su_s, in_=su2), incs=[("lx", 16)])
        S.emit("sync", lambda e: e.dma_start(out=sd_s, in_=sd2), incs=[("lx", 16)])
        S.emit("vector", lambda e: e.memset(ones128, 1.0), incs=[("sv", 1)])
        S.emit("vector", lambda e: e.memset(negc_row, -128.0), incs=[("sv", 1)])
        S.emit("vector", lambda e: e.memset(onescol_bf, 1.0), incs=[("sv", 1)])

        def ln_stats(src3, src_is_f32, T=TB, src_w=()):
            onec = onescol_bf
            st_b = S.ctr["st"]
            if src_is_f32:
                sv_b, ss_b = S.ctr["sv"], S.ctr["ss"]
                for c in range(C):
                    scr_w = [("st", st_b + c - 1)] if c >= 2 else []
                    ex = list(src_w) if c == 0 else []
                    S.emit("vector",
                           lambda e, c=c: e.tensor_copy(xbs[c % 2][:, :T], src3[:, c, :]),
                           waits=scr_w + ex, incs=[("sv", 1)])
                    S.emit("scalar",
                           lambda e, c=c: e.activation(sqb[c % 2][:, :T], src3[:, c, :], AF.Square),
                           waits=scr_w + ex, incs=[("ss", 1)])
                    S.emit("tensor",
                           lambda e, c=c: e.matmul(P[5][0:1, :T], onec, xbs[c % 2][:, :T],
                                                   start=(c == 0), stop=(c == C - 1)),
                           waits=[("sv", sv_b + c + 1), ("ss", ss_b + c + 1)])
                    S.emit("tensor",
                           lambda e, c=c: e.matmul(P[6][0:1, :T], onec, sqb[c % 2][:, :T],
                                                   start=(c == 0), stop=(c == C - 1)),
                           incs=[("st", 1)])
            else:
                ss_b = S.ctr["ss"]
                for c in range(C):
                    scr_w = [("st", st_b + c - 1)] if c >= 2 else []
                    S.emit("scalar",
                           lambda e, c=c: e.activation(sqb[c % 2][:, :T], src3[:, c, :], AF.Square),
                           waits=scr_w, incs=[("ss", 1)])
                    S.emit("tensor",
                           lambda e, c=c: e.matmul(P[5][0:1, :T], onec, src3[:, c, :],
                                                   start=(c == 0), stop=(c == C - 1)))
                    S.emit("tensor",
                           lambda e, c=c: e.matmul(P[6][0:1, :T], onec, sqb[c % 2][:, :T],
                                                   start=(c == 0), stop=(c == C - 1)),
                           waits=[("ss", ss_b + c + 1)], incs=[("st", 1)])
            wt = S.w("st")
            S.emit("vector", lambda e: e.tensor_scalar(rm[:, :T], P[5][0:1, :T], 1.0 / D, None, OP.mult),
                   waits=wt)
            S.emit("vector", lambda e: e.tensor_scalar(rowA[:, :T], P[6][0:1, :T], 1.0 / D, EPS,
                                                       OP.mult, OP.add))
            S.emit("vector", lambda e: e.tensor_tensor(rowB[:, :T], rm[:, :T], rm[:, :T], OP.mult))
            S.emit("vector", lambda e: e.tensor_tensor(rowA[:, :T], rowA[:, :T], rowB[:, :T], OP.subtract),
                   incs=[("sv", 1)])
            S.emit("scalar", lambda e: e.activation(rowB[:, :T], rowA[:, :T], AF.Sqrt),
                   waits=S.w("sv"), incs=[("ss", 1)])
            S.emit("vector", lambda e: e.reciprocal(rr[:, :T], rowB[:, :T]),
                   waits=S.w("ss"), incs=[("sv", 1)])
            wv_ = S.w("sv")
            S.emit("tensor", lambda e: e.matmul(P[2][:, :T], ones128, rm[:, :T],
                                                start=True, stop=True), waits=wv_)
            S.emit("tensor", lambda e: e.matmul(P[3][:, :T], ones128, rr[:, :T],
                                                start=True, stop=True), incs=[("st", 1)])

        def bcast_evac(T=TB):
            return

        def ln_apply(src3, dst3, T=TB):
            bcast_evac(T)
            wt = S.w("st")
            wg = S.w("ss")
            for c in range(NSPLIT):
                S.emit("vector", lambda e, c=c: e.tensor_tensor(tsc[:, :T], src3[:, c, :], P[2][:, :T], OP.subtract),
                       waits=wt if c == 0 else [])
                S.emit("vector", lambda e, c=c: e.tensor_tensor(dst3[:, c, :], tsc[:, :T], P[3][:, :T], OP.mult),
                       incs=[("sv", 1)])
            for c in range(NSPLIT, C):
                S.emit("gpsimd", lambda e, c=c: e.tensor_tensor(tsc2[:, :T], src3[:, c, :], mb_sb[:, :T], OP.subtract),
                       waits=wg if c == NSPLIT else [])
                S.emit("gpsimd", lambda e, c=c: e.tensor_tensor(dst3[:, c, :], tsc2[:, :T], rbx_sb[:, :T], OP.mult),
                       incs=[("sg", 1)])

        def ln_residual(src3, T=TB):
            bcast_evac(T)
            wt = S.w("st")
            wg = S.w("ss")
            for c in range(NSPLIT):
                S.emit("vector", lambda e, c=c: e.tensor_tensor(tsc[:, :T], src3[:, c, :], P[2][:, :T], OP.subtract),
                       waits=wt if c == 0 else [])
                S.emit("vector", lambda e, c=c: e.tensor_tensor(tsc[:, :T], tsc[:, :T], P[3][:, :T], OP.mult))
                S.emit("vector", lambda e, c=c: e.tensor_tensor(XB[:, c, :T], XB[:, c, :T], tsc[:, :T], OP.add),
                       incs=[("sv", 1)])
            for c in range(NSPLIT, C):
                S.emit("gpsimd", lambda e, c=c: e.tensor_tensor(tsc2[:, :T], src3[:, c, :], mb_sb[:, :T], OP.subtract),
                       waits=wg if c == NSPLIT else [])
                S.emit("gpsimd", lambda e, c=c: e.tensor_tensor(tsc2[:, :T], tsc2[:, :T], rbx_sb[:, :T], OP.mult))
                S.emit("gpsimd", lambda e, c=c: e.tensor_tensor(XB[:, c, :T], XB[:, c, :T], tsc2[:, :T], OP.add),
                       incs=[("sg", 1)])

        S.block()
        for t in range(NTILES):
            t0 = t * TB
            S.emit("sync", lambda e, t0=t0: e.dma_start(out=XB[:, :, :], in_=xt3[:, :, t0:t0 + TB]),
                   waits=S.w("sv", "sg"), incs=[("lx", 16)])
            wx = S.w("lx")
            S.emit("vector", lambda e: e.nop(), waits=wx)
            S.emit("scalar", lambda e: e.nop(), waits=wx)
            ln_stats(XB, True)
            ln_apply(XB, h1a[:, :, t0:t0 + TB])

        S.block()
        for fb in range(4):
            wfree = S.w("st")
            S.emit("gpsimd", lambda e, fb=fb: e.dma_start(out=wkb, in_=wk3[:, :, fb * 512:(fb + 1) * 512]),
                   waits=wfree, incs=[("lg", 16)])
            S.emit("gpsimd", lambda e, fb=fb: e.dma_start(out=wvb, in_=wv3[:, :, fb * 512:(fb + 1) * 512]),
                   incs=[("lg", 16)])
            wl = S.w("lg")
            ss_b, sv_b = S.ctr["ss"], S.ctr["sv"]
            for g in range(C):
                pk = P[0] if g % 2 == 0 else P[1]
                pv = P[2] if g % 2 == 0 else P[3]
                if g == 0:
                    first_w = wl
                elif g >= 2:
                    first_w = [("ss", ss_b + g - 1), ("sv", sv_b + g - 1)]
                else:
                    first_w = []
                for c in range(C):
                    S.emit("tensor", lambda e, c=c, g=g, pk=pk: e.matmul(
                        pk[:, :], h1a[:, c, fbg(fb, g)], wkb[:, c, :],
                        start=(c == 0), stop=(c == C - 1)),
                        waits=(first_w if c == 0 else []),
                        incs=([("st", 1)] if c == C - 1 else []))
                wt_k = S.w("st")
                for c in range(C):
                    S.emit("tensor", lambda e, c=c, g=g, pv=pv: e.matmul(
                        pv[:, :], h1a[:, c, fbg(fb, g)], wvb[:, c, :],
                        start=(c == 0), stop=(c == C - 1)),
                        incs=([("st", 1)] if c == C - 1 else []))
                wt_v = S.w("st")
                S.emit("scalar", lambda e, g=g, pk=pk: e.activation(kfb[:, g, :], pk[:, :], AF.Relu),
                       waits=wt_k, incs=[("ss", 1)])
                S.emit("vector", lambda e, g=g, pv=pv: e.tensor_copy(vfb[:, g, :], pv[:, :]),
                       waits=wt_v, incs=[("sv", 1)])
            ev_all = [("ss", S.ctr["ss"]), ("sv", S.ctr["sv"])]
            for hp in range(4):
                for g in range(C):
                    S.emit("tensor", lambda e, g=g, hp=hp: e.matmul(
                        P[4][:, hp * 128:(hp + 1) * 128],
                        kfb[:, g, hp * 128:(hp + 1) * 128],
                        vfb[:, g, hp * 128:(hp + 1) * 128],
                        start=(g == 0), stop=(g == C - 1)),
                        waits=(ev_all if (hp == 0 and g == 0) else []))
                    S.emit("tensor", lambda e, g=g, hp=hp: e.matmul(
                        P[5][:, hp:hp + 1],
                        kfb[:, g, hp * 128:(hp + 1) * 128], onescol_bf,
                        start=(g == 0), stop=(g == C - 1)),
                        incs=([("st", 1)] if (hp == 3 and g == C - 1) else []))
            wt = S.w("st")
            for hp in range(4):
                col = (fb * 4 + hp) * 65
                S.emit("vector", lambda e, hp=hp, col=col: e.tensor_copy(
                    kvks_sb[0:64, col:col + 64], P[4][0:64, hp * 128:hp * 128 + 64]),
                    waits=(wt if hp == 0 else []))
                S.emit("vector", lambda e, hp=hp, col=col: e.tensor_copy(
                    kvks_sb[64:128, col:col + 64], P[4][64:128, hp * 128 + 64:hp * 128 + 128]))
                S.emit("vector", lambda e, hp=hp, col=col: e.tensor_copy(
                    kvks_sb[:, col + 64:col + 65], P[5][:, hp:hp + 1]),
                    incs=[("sv", 1)])
            S.emit("tensor", lambda e: e.nop(), waits=S.w("sv"))

        S.block()
        S.emit("gpsimd", lambda e: e.dma_start(out=cc_in.ap(), in_=kvks_sb), waits=S.w("sv"),
               incs=[("lg", 16)])
        S.emit("gpsimd", lambda e: e.collective_compute(
            "AllReduce", OP.add, replica_groups=[list(range(NCORES))],
            ins=[cc_in.ap().opt()], outs=[cc_out.ap().opt()]),
            waits=S.w("lg"), incs=[("cc", 1)])
        S.emit("gpsimd", lambda e: e.dma_start(out=kvks_sb, in_=cc_out.ap()), waits=S.w("cc"),
               incs=[("lg", 16)])
        S.emit("vector", lambda e: e.tensor_copy(kvks_bf, kvks_sb), waits=S.w("lg"),
               incs=[("sv", 1)])

        for t in range(NTILES):
            t0 = t * TB
            S.block()
            S.emit("sync", lambda e, t0=t0: e.dma_start(out=XB[:, :, :], in_=xt3[:, :, t0:t0 + TB]),
                   waits=[("so", 16 * t)], incs=[("lx", 16)])
            wx = S.w("lx")
            S.emit("vector", lambda e: e.nop(), waits=wx)
            S.emit("scalar", lambda e: e.nop(), waits=wx)

            def wload(dst, raw_src, cache_src, waits):
                if t == 0:
                    S.emit("gpsimd", lambda e, dst=dst, s=raw_src: e.dma_start(out=dst, in_=s),
                           waits=waits, incs=[("lg", 16)])
                    dep = ("lg", S.ctr["lg"])
                    S.emit("sync", lambda e, dst=dst, s=cache_src: e.dma_start(out=s, in_=dst),
                           waits=[dep], incs=[("sc", 16)])
                    return dep, S.ctr["sc"]
                else:
                    S.emit("sync", lambda e, dst=dst, s=cache_src: e.dma_start(out=dst, in_=s),
                           waits=waits, incs=[("lx", 16)])
                    return ("lx", S.ctr["lx"]), 0

            tile_st0 = S.ctr["st"]
            sc_full = S.ctr["sc"]
            ln_stats(XB, True)
            ln_apply(XB, hbuf)
            h1_done = S.w("sv", "sg")

            wq_free = []
            wq_sc = []
            ss_b = S.ctr["ss"]
            for fc in range(FC):
                fw = [("st", wq_free[fc - 2])] if fc >= 2 else [("st", tile_st0)]
                if t == 0 and fc >= 2:
                    fw.append(("sc", wq_sc[fc - 2]))
                if t == 1 and fc == 0:
                    fw.append(("sc", sc_full))
                wb = w1 if fc % 2 == 0 else w2
                dep, csc = wload(wb, wq_t[fc], acache[fc], fw)
                wq_sc.append(csc)
                pq = P[0] if fc % 2 == 0 else P[1]
                first_w = [dep]
                if fc == 0:
                    first_w += h1_done
                if fc >= 2:
                    first_w += [("ss", ss_b + fc - 1)]
                for c in range(C):
                    S.emit("tensor", lambda e, c=c, pq=pq, wb=wb: e.matmul(
                        pq[:, :], wb[:, c, :], hbuf[:, c, :],
                        start=(c == 0), stop=(c == C - 1)),
                        waits=(first_w if c == 0 else []),
                        incs=([("st", 1)] if c == C - 1 else []))
                wq_free.append(S.ctr["st"])
                S.emit("scalar", lambda e, fc=fc, pq=pq: e.activation(qo[:, fc, :], pq[:, :], AF.Relu),
                       waits=S.w("st"), incs=[("ss", 1)])

            wq_done = S.w("ss")
            for tc in range(4):
                for hp in range(NHP):
                    col0 = tc * 32 + 2 * hp
                    S.emit("tensor", lambda e, tc=tc, hp=hp, col0=col0: e.matmul(
                        P[4][:, col0:col0 + 1],
                        qo[0:64, hp, tc * 128:(tc + 1) * 128],
                        kvks_bf[0:64, hp * 65 + 64:hp * 65 + 65],
                        start=True, stop=True),
                        waits=(wq_done if (tc == 0 and hp == 0) else []))
                    S.emit("tensor", lambda e, tc=tc, hp=hp, col0=col0: e.matmul(
                        P[4][:, col0 + 1:col0 + 2],
                        qo[64:128, hp, tc * 128:(tc + 1) * 128],
                        kvks_bf[64:128, hp * 65 + 64:hp * 65 + 65],
                        start=True, stop=True, tile_position=(64, 0)),
                        incs=([("st", 1)] if (tc == 3 and hp == NHP - 1) else []))
            S.emit("vector", lambda e: e.tensor_scalar(rn_tok, P[4][:, 0:128], 1e-6, None, OP.max),
                   waits=S.w("st"))
            S.emit("vector", lambda e: e.reciprocal(rn_tok, rn_tok))
            for tc in range(4):
                for g in range(4):
                    S.emit("vector", lambda e, tc=tc, g=g: e.transpose(
                        rn_f[0:32, tc * 128 + g * 32:tc * 128 + (g + 1) * 32],
                        rn_tok[g * 32:(g + 1) * 32, tc * 32:(tc + 1) * 32]),
                        incs=([("sv", 1)] if (tc == 3 and g == 3) else []))

            rn_ready = S.w("sv")
            sv_b, ss_b = S.ctr["sv"], S.ctr["ss"]
            for hp in range(NHP):
                pq = P[0] if hp % 2 == 0 else P[1]
                pr = P[2] if hp % 2 == 0 else P[3]
                first_w = list(rn_ready) if hp == 0 else []
                if hp >= 2:
                    first_w += [("sv", sv_b + hp - 1)]
                S.emit("tensor", lambda e, hp=hp, pq=pq: e.matmul(
                    pq[0:64, :], kvks_bf[0:64, hp * 65:hp * 65 + 64], qo[0:64, hp, :],
                    start=True, stop=True), waits=first_w)
                S.emit("tensor", lambda e, hp=hp, pq=pq: e.matmul(
                    pq[64:128, :], kvks_bf[64:128, hp * 65:hp * 65 + 64], qo[64:128, hp, :],
                    start=True, stop=True, tile_position=(64, 64)))
                S.emit("tensor", lambda e, hp=hp, pr=pr: e.matmul(
                    pr[:, :], sel32[:, hp * 128:(hp + 1) * 128], rn_f,
                    start=True, stop=True),
                    incs=[("st", 1)])
                sw = [("sv", sv_b + hp - 1)] if hp >= 2 else []
                S.emit("scalar", lambda e, hp=hp, pr=pr: e.activation(rb_sb[hp % 2], pr[:, :], AF.Identity),
                       waits=S.w("st") + sw, incs=[("ss", 1)])
                we = S.w("ss")
                S.emit("vector", lambda e, hp=hp, pq=pq: e.tensor_tensor(
                    aiy[0:64, hp, :], pq[0:64, :], rb_sb[hp % 2][0:64, :], OP.mult), waits=we)
                S.emit("vector", lambda e, hp=hp, pq=pq: e.tensor_tensor(
                    aiy[64:128, hp, :], pq[64:128, :], rb_sb[hp % 2][64:128, :], OP.mult),
                    incs=[("sv", 1)])

            ai_done = S.w("sv")
            wo_free = []
            wo_sc = []
            ss_b = S.ctr["ss"]
            for fc in range(FC):
                fw = [("st", wq_free[-1])] if fc < 2 else [("st", wo_free[fc - 2])]
                if t == 0:
                    fw.append(("sc", wq_sc[-1] if fc < 2 else wo_sc[fc - 2]))
                wb = w1 if fc % 2 == 0 else w2
                dep, csc = wload(wb, wo_t[fc], acache[FC + fc], fw)
                wo_sc.append(csc)
                po = P[0] if fc % 2 == 0 else P[1]
                first_w = [dep]
                if fc == 0:
                    first_w += ai_done
                if fc >= 2:
                    first_w += [("ss", ss_b + fc - 1)]
                for c in range(C):
                    S.emit("tensor", lambda e, c=c, po=po, wb=wb: e.matmul(
                        po[:, :], wb[:, c, :], aiy[:, c, :],
                        start=(c == 0), stop=(c == C - 1)),
                        waits=(first_w if c == 0 else []),
                        incs=([("st", 1)] if c == C - 1 else []))
                wo_free.append(S.ctr["st"])
                S.emit("scalar", lambda e, fc=fc, po=po: e.activation(
                    qo[:, fc, :], po[:, :], AF.Identity, bias=bo_s[:, fc:fc + 1]),
                    waits=S.w("st"), incs=[("ss", 1)])

            S.emit("tensor", lambda e: e.nop(), waits=S.w("ss"))
            ln_stats(qo, False)
            ln_residual(qo)
            resid1_w = S.w("sv", "sg")

            ln_stats(XB, True, src_w=S.w("sg"))
            ln_apply(XB, hbuf)

            h2_done = S.w("sv", "sg")
            for c in range(C):
                S.emit("tensor", lambda e, c=c: e.matmul(
                    P[5][0:1, :], onescol_bf, hbuf[:, c, :],
                    start=(c == 0), stop=(c == C - 1)),
                    waits=(h2_done if c == 0 else []),
                    incs=([("st", 1)] if c == C - 1 else []))
            S.emit("vector", lambda e: e.tensor_copy(h2s_row, P[5][0:1, :]), waits=S.w("st"),
                   incs=[("sv", 1)])

            h2s_ready = S.w("sv")
            gfree = []
            gu_sc = []
            ss_b, sv_b = S.ctr["ss"], S.ctr["sv"]
            for hc in range(HC):
                fw = [("st", gfree[hc - 2])] if hc >= 2 else [("st", tile_st0)]
                if t == 0 and hc >= 2:
                    fw.append(("sc", gu_sc[hc - 2]))
                wload(w3[hc % 2], g_t[hc], gcache[hc], fw)
                dep, _ = wload(w4[hc % 2], u_t[hc], ucache[hc], [])
                gu_sc.append(S.ctr["sc"])
                if hc == 7:
                    dw0_dep, dw0_csc = wload(dw[0], d_t[0], dcache[0],
                                             list(resid1_w))
                pg = P[0] if hc % 2 == 0 else P[2]
                pu = P[1] if hc % 2 == 0 else P[3]
                first_w = [dep]
                if hc == 0:
                    first_w += h2s_ready
                if hc >= 2:
                    first_w += [("ss", ss_b + hc - 1), ("sv", sv_b + hc - 1)]
                if hc >= 2:
                    S.emit("tensor", lambda e, hc=hc: e.matmul(
                        P[6][0:1, :], onescol_bf, gu[:, hc - 2, :],
                        start=(hc - 2 == 0), stop=False),
                        waits=[("sv", sv_b + hc - 1)])
                for c in range(C):
                    S.emit("tensor", lambda e, c=c, hc=hc, pg=pg: e.matmul(
                        pg[:, :], w3[hc % 2][:, c, :], hbuf[:, c, :],
                        start=(c == 0), stop=False),
                        waits=(first_w if c == 0 else []))
                S.emit("tensor", lambda e, pg=pg: e.matmul(
                    pg[:, :], negc_row, h2s_row, start=False, stop=True))
                for c in range(C):
                    S.emit("tensor", lambda e, c=c, hc=hc, pu=pu: e.matmul(
                        pu[:, :], w4[hc % 2][:, c, :], hbuf[:, c, :],
                        start=(c == 0), stop=False))
                S.emit("tensor", lambda e, pu=pu: e.matmul(
                    pu[:, :], negc_row, h2s_row, start=False, stop=True),
                    incs=[("st", 1)])
                gfree.append(S.ctr["st"])
                wt = S.w("st")
                S.emit("scalar", lambda e, hc=hc, pg=pg: e.activation(
                    g_tmp[hc % 2], pg[:, :], AF.Silu, scale=sg_s[:, hc:hc + 1]),
                    waits=wt, incs=[("ss", 1)])
                S.emit("vector", lambda e, hc=hc, pu=pu: e.scalar_tensor_tensor(
                    gu[:, hc, :], pu[:, :], su_s[:, hc:hc + 1], g_tmp[hc % 2],
                    OP.mult, OP.mult),
                    waits=S.w("ss"), incs=[("sv", 1)])
            for hc in (HC - 2, HC - 1):
                S.emit("tensor", lambda e, hc=hc: e.matmul(
                    P[6][0:1, :], onescol_bf, gu[:, hc, :],
                    start=False, stop=(hc == HC - 1)),
                    waits=[("sv", sv_b + hc + 1)],
                    incs=([("st", 1)] if hc == HC - 1 else []))
            S.emit("vector", lambda e: e.tensor_copy(gus_row, P[6][0:1, :]), waits=S.w("st"),
                   incs=[("sv", 1)])

            gus_ready = S.w("sv")
            dfree = []
            dn_sc = []
            ss_b = S.ctr["ss"]
            for oc in range(FC):
                if oc == 0:
                    dep, csc = dw0_dep, dw0_csc
                else:
                    if oc == 1:
                        fw = [("st", gfree[-1])]
                    else:
                        fw = [("st", dfree[oc - 2])]
                        if t == 0:
                            fw.append(("sc", dn_sc[oc - 2]))
                    dep, csc = wload(dw[oc % 2], d_t[oc], dcache[oc], fw)
                dn_sc.append(csc)
                pd = P[0] if oc % 2 == 0 else P[1]
                first_w = [dep]
                if oc == 0:
                    first_w += gus_ready
                if oc >= 2:
                    first_w += [("ss", ss_b + oc - 1)]
                for c in range(HC):
                    S.emit("tensor", lambda e, c=c, oc=oc, pd=pd: e.matmul(
                        pd[:, :], dw[oc % 2][:, c, :], gu[:, c, :],
                        start=(c == 0), stop=False),
                        waits=(first_w if c == 0 else []))
                S.emit("tensor", lambda e, pd=pd: e.matmul(
                    pd[:, :], negc_row, gus_row, start=False, stop=True),
                    incs=[("st", 1)])
                dfree.append(S.ctr["st"])
                S.emit("scalar", lambda e, oc=oc, pd=pd: e.activation(
                    aiy[:, oc, :], pd[:, :], AF.Identity, scale=sd_s[:, oc:oc + 1]),
                    waits=S.w("st"), incs=[("ss", 1)])

            S.emit("tensor", lambda e: e.nop(), waits=S.w("ss"))
            ln_stats(aiy, False)
            ln_residual(aiy)

            S.emit("sync", lambda e, t0=t0: e.dma_start(out=out3[:, :, t0:t0 + TB], in_=XB[:, :, :]),
                   waits=S.w("sv", "sg"), incs=[("so", 16)])

        S.block()
        S.emit("sync", lambda e: e.nop(), waits=[("so", 16 * NTILES)])

        S.realize(nc, sems)
    return nc


def fbg(fb, g):
    return slice(g * 128, (g + 1) * 128)


_NC_CACHE = None


def _get_program():
    global _NC_CACHE
    if _NC_CACHE is None:
        _NC_CACHE = build_program()
    return _NC_CACHE


def kernel(L0_attn_hidden_states, L0_attn_Wq, L0_attn_Wk, L0_attn_Wv, L0_attn_Wo,
           L0_attn_bo, L0_ffn_wq_gate, L0_ffn_s_gate, L0_ffn_wq_up, L0_ffn_s_up,
           L0_ffn_wq_down, L0_ffn_s_down):
    x = np.asarray(L0_attn_hidden_states)
    S_FULL = x.shape[1]
    xT = np.ascontiguousarray(x[0].T)

    def tile_proj(W):
        return np.ascontiguousarray(
            np.asarray(W, dtype=np.float32).reshape(C, 128, FC, 128).transpose(2, 1, 0, 3))

    wq_t = tile_proj(L0_attn_Wq)
    wo_t = tile_proj(L0_attn_Wo)
    wk = np.ascontiguousarray(np.asarray(L0_attn_Wk, dtype=np.float32))
    wv = np.ascontiguousarray(np.asarray(L0_attn_Wv, dtype=np.float32))

    def tile_quant(Wq_int, out_chunks):
        W = np.asarray(Wq_int, dtype=np.int32)
        O, I = W.shape
        WT = np.ascontiguousarray(W.T)
        return np.ascontiguousarray(
            WT.reshape(I // 128, 128, out_chunks, 128).transpose(2, 1, 0, 3))

    g_t = tile_quant(L0_ffn_wq_gate, HC)
    u_t = tile_quant(L0_ffn_wq_up, HC)
    d_t = tile_quant(L0_ffn_wq_down, FC)

    def col2(v, nch):
        return np.ascontiguousarray(np.asarray(v, dtype=np.float32).reshape(nch, 128).T)

    bo2 = col2(L0_attn_bo, C)
    sg2 = col2(L0_ffn_s_gate, HC)
    su2 = col2(L0_ffn_s_up, HC)
    sd2 = col2(L0_ffn_s_down, C)

    sel32 = np.zeros((32, NHP * 128), dtype=np.float32)
    for hp in range(NHP):
        sel32[2 * hp, hp * 128:hp * 128 + 64] = 1.0
        sel32[2 * hp + 1, hp * 128 + 64:(hp + 1) * 128] = 1.0

    nc = _get_program()
    shared = {"wq_t": wq_t, "wo_t": wo_t, "wk": wk, "wv": wv, "g_t": g_t,
              "u_t": u_t, "d_t": d_t, "bo2": bo2, "sg2": sg2, "su2": su2,
              "sd2": sd2, "sel32": sel32}
    in_maps = []
    for c in range(NCORES):
        m = dict(shared)
        m["xt"] = np.ascontiguousarray(xT[:, c * TOK:(c + 1) * TOK])
        in_maps.append(m)

    res = run_bass_kernel_spmd(nc, in_maps, core_ids=list(range(NCORES)))
    out = np.empty((D, S_FULL), dtype=np.float32)
    for c in range(NCORES):
        out[:, c * TOK:(c + 1) * TOK] = res.results[c]["out_t"]
    return np.ascontiguousarray(out.T).reshape(1, S_FULL, D)


if __name__ == "__main__":
    print("building program...")
    nc = build_program()
    n = sum(1 for _ in nc.all_instructions())
    print("instructions:", n)


# version 18
# speedup vs baseline: 1.0433x; 1.0230x over previous
import contextlib
import numpy as np
import concourse.bass as bass
import concourse.mybir as mybir
from concourse.bass_utils import run_bass_kernel_spmd

F32 = mybir.dt.float32
BF16 = mybir.dt.bfloat16
I32 = mybir.dt.int32
F32R = mybir.dt.float32r
AF = mybir.ActivationFunctionType
OP = mybir.AluOpType

NCORES = 8
TOK = 2048
TB = 512
NTILES = TOK // TB
C = 16
D = 2048
HID = 5632
HC = 44
FC = 16
NHP = 16
EPS = 1e-5

ENGINES = ("sync", "gpsimd", "vector", "scalar", "tensor")
SEMS = ("lx", "lg", "sv", "ss", "st", "so", "cc", "sc", "sg")
NSPLIT = 16


class Sched:
    def __init__(self):
        self.blocks = []
        self.cur = None
        self.ctr = {s: 0 for s in SEMS}
        self.block()

    def block(self):
        self.cur = {}
        self.blocks.append(self.cur)

    def emit(self, eng, fn, waits=(), incs=()):
        for s, n in incs:
            self.ctr[s] += n
        self.cur.setdefault(eng, []).append((tuple(waits), fn, tuple(incs)))

    def w(self, *sems):
        return [(s, self.ctr[s]) for s in sems if self.ctr[s] > 0]

    def realize(self, nc, sems):
        for blk in self.blocks:
            if not blk:
                continue
            with nc.Block() as B:
                for eng_name, items in blk.items():
                    def body(eng, items=items):
                        for waits, fn, incs in items:
                            for s, v in waits:
                                if v > 0:
                                    eng.wait_ge(sems[s], v)
                            inst = fn(eng)
                            for s, n in incs:
                                inst.then_inc(sems[s], n)
                    getattr(B, eng_name)(body)


def build_program():
    nc = bass.Bass("TRN2", target_bir_lowering=False, detect_race_conditions=False)

    xt = nc.dram_tensor("xt", [D, TOK], F32, kind="ExternalInput").ap()
    wq_t = nc.dram_tensor("wq_t", [FC, 128, C, 128], F32, kind="ExternalInput").ap()
    wo_t = nc.dram_tensor("wo_t", [FC, 128, C, 128], F32, kind="ExternalInput").ap()
    wk = nc.dram_tensor("wk", [D, D], F32, kind="ExternalInput").ap()
    wv = nc.dram_tensor("wv", [D, D], F32, kind="ExternalInput").ap()
    g_t = nc.dram_tensor("g_t", [HC, 128, C, 128], I32, kind="ExternalInput").ap()
    u_t = nc.dram_tensor("u_t", [HC, 128, C, 128], I32, kind="ExternalInput").ap()
    d_t = nc.dram_tensor("d_t", [FC, 128, HC, 128], I32, kind="ExternalInput").ap()
    bo2 = nc.dram_tensor("bo2", [128, C], F32, kind="ExternalInput").ap()
    sg2 = nc.dram_tensor("sg2", [128, HC], F32, kind="ExternalInput").ap()
    su2 = nc.dram_tensor("su2", [128, HC], F32, kind="ExternalInput").ap()
    sd2 = nc.dram_tensor("sd2", [128, C], F32, kind="ExternalInput").ap()
    sel32_d = nc.dram_tensor("sel32", [32, NHP * 128], F32, kind="ExternalInput").ap()
    out_t = nc.dram_tensor("out_t", [D, TOK], F32, kind="ExternalOutput").ap()
    cc_in = nc.dram_tensor("cc_in", [128, NHP * 65], F32)
    cc_out = nc.dram_tensor("cc_out", [128, NHP * 65], F32)
    gcache = nc.dram_tensor("gcache", [HC, 128, C * 128], BF16).ap()
    ucache = nc.dram_tensor("ucache", [HC, 128, C * 128], BF16).ap()
    dcache = nc.dram_tensor("dcache", [FC, 128, HC * 128], BF16).ap()
    acache = nc.dram_tensor("acache", [2 * FC, 128, C * 128], BF16).ap()

    xt3 = xt.rearrange("(c p) t -> p c t", p=128)
    out3 = out_t.rearrange("(c p) t -> p c t", p=128)
    wk3 = wk.rearrange("(c p) f -> p c f", p=128)
    wv3 = wv.rearrange("(c p) f -> p c f", p=128)

    S = Sched()

    with contextlib.ExitStack() as ctx:
        E = ctx.enter_context
        ABF = E(nc.sbuf_tensor("ABF", [128, 70848], BF16)).ap()
        AF_ = E(nc.sbuf_tensor("AF", [128, 13472], F32)).ap()
        sel32 = E(nc.sbuf_tensor("sel32_s", [32, NHP * 128], F32)).ap()
        ones128 = E(nc.sbuf_tensor("ones128", [1, 128], F32)).ap()
        rn_f = E(nc.sbuf_tensor("rn_f", [32, TB], F32)).ap()
        bo_s = E(nc.sbuf_tensor("bo_s", [128, C], F32)).ap()
        sg_s = E(nc.sbuf_tensor("sg_s", [128, HC], F32)).ap()
        su_s = E(nc.sbuf_tensor("su_s", [128, HC], F32)).ap()
        sd_s = E(nc.sbuf_tensor("sd_s", [128, C], F32)).ap()
        P = [E(nc.psum_tensor(f"P{i}", [128, 512], F32)).ap() for i in range(8)]
        sems = {s: E(nc.semaphore(s)) for s in SEMS}

        def bfv(off, n):
            return ABF[:, off:off + n]

        kvks_bf = bfv(0, 1040)
        xbs = [bfv(1056, 512), bfv(1568, 512)]
        sqb = [bfv(2080, 512), bfv(2592, 512)]
        g_tmp = [bfv(3104, 512), bfv(3616, 512)]
        h2s_row = ABF[0:1, 4128:4640]
        gus_row = ABF[0:1, 4640:5152]
        negc_row = ABF[0:1, 5152:5280]
        onescol_bf = ABF[:, 5280:5281]
        RB = 5312
        h1a = bfv(RB, C * TOK).rearrange("p (c t) -> p c t", t=TOK)
        kfb = bfv(RB + 32768, C * 512).rearrange("p (g f) -> p g f", f=512)
        vfb = bfv(RB + 40960, C * 512).rearrange("p (g f) -> p g f", f=512)
        wkb = bfv(RB + 49152, C * 512).rearrange("p (c f) -> p c f", f=512)
        wvb = bfv(RB + 57344, C * 512).rearrange("p (c f) -> p c f", f=512)
        hbuf = bfv(RB, C * TB).rearrange("p (c t) -> p c t", t=TB)
        qo = bfv(RB + 8192, C * TB).rearrange("p (c t) -> p c t", t=TB)
        aiy = bfv(RB + 16384, C * TB).rearrange("p (c t) -> p c t", t=TB)
        gu = bfv(RB + 24576, HC * TB).rearrange("p (c t) -> p c t", t=TB)
        w1 = bfv(RB + 47104, C * 128).rearrange("p (c f) -> p c f", f=128)
        w2 = bfv(RB + 49152, C * 128).rearrange("p (c f) -> p c f", f=128)
        w3 = [bfv(RB + 51200, C * 128).rearrange("p (c f) -> p c f", f=128),
              bfv(RB + 53248, C * 128).rearrange("p (c f) -> p c f", f=128)]
        w4 = [bfv(RB + 55296, C * 128).rearrange("p (c f) -> p c f", f=128),
              bfv(RB + 57344, C * 128).rearrange("p (c f) -> p c f", f=128)]
        dw = [bfv(RB + 8192, HC * 128).rearrange("p (c f) -> p c f", f=128),
              bfv(RB, HC * 128).rearrange("p (c f) -> p c f", f=128)]

        def fv(off, n):
            return AF_[:, off:off + n]

        kvks_sb = fv(0, 1040)
        rm = AF_[0:1, 1040:1552]
        rr = AF_[0:1, 1552:2064]
        rowA = AF_[0:1, 2064:2576]
        rowB = AF_[0:1, 2576:3088]
        rb_sb = [fv(3088, 512), fv(3600, 512)]
        onescol_f32 = AF_[:, 13440:13441]
        rn_tok = fv(4112, 128)
        tsc = fv(4240, 512)
        XB = fv(4768, C * 512).rearrange("p (c t) -> p c t", t=512)
        tsc2 = fv(12960, 512)
        mb_sb = kvks_sb[:, 0:512]
        rbx_sb = kvks_sb[:, 512:1024]

        S.emit("sync", lambda e: e.dma_start(out=sel32, in_=sel32_d), incs=[("lx", 16)])
        S.emit("sync", lambda e: e.dma_start(out=bo_s, in_=bo2), incs=[("lx", 16)])
        S.emit("sync", lambda e: e.dma_start(out=sg_s, in_=sg2), incs=[("lx", 16)])
        S.emit("sync", lambda e: e.dma_start(out=su_s, in_=su2), incs=[("lx", 16)])
        S.emit("sync", lambda e: e.dma_start(out=sd_s, in_=sd2), incs=[("lx", 16)])
        S.emit("vector", lambda e: e.memset(ones128, 1.0), incs=[("sv", 1)])
        S.emit("vector", lambda e: e.memset(negc_row, -128.0), incs=[("sv", 1)])
        S.emit("vector", lambda e: e.memset(onescol_bf, 1.0), incs=[("sv", 1)])
        S.emit("vector", lambda e: e.memset(onescol_f32, 1.0), incs=[("sv", 1)])

        def ln_stats(src3, src_is_f32, T=TB, src_w=(), chunk_gate=None):
            onec = onescol_bf
            st_b = S.ctr["st"]
            if src_is_f32:
                ss_b = S.ctr["ss"]
                for c in range(C):
                    scr_w = [("st", st_b + c - 1)] if c >= 2 else []
                    ex = list(src_w) if c == 0 else []
                    if chunk_gate is not None:
                        ex = ex + [(chunk_gate[0], chunk_gate[1] + chunk_gate[2] * (c + 1))]
                    S.emit("scalar",
                           lambda e, c=c: e.activation(sqb[c % 2][:, :T], src3[:, c, :], AF.Square),
                           waits=scr_w + ex, incs=[("ss", 1)])
                    S.emit("tensor",
                           lambda e, c=c: e.matmul(P[5][0:1, :T], onescol_f32, src3[:, c, :],
                                                   start=(c == 0), stop=(c == C - 1)),
                           waits=ex if c == 0 else (ex if chunk_gate is not None else []))
                    S.emit("tensor",
                           lambda e, c=c: e.matmul(P[7][0:1, :T], onec, sqb[c % 2][:, :T],
                                                   start=(c == 0), stop=(c == C - 1)),
                           waits=[("ss", ss_b + c + 1)], incs=[("st", 1)])
            else:
                ss_b = S.ctr["ss"]
                for c in range(C):
                    scr_w = [("st", st_b + c - 1)] if c >= 2 else []
                    S.emit("scalar",
                           lambda e, c=c: e.activation(sqb[c % 2][:, :T], src3[:, c, :], AF.Square),
                           waits=scr_w, incs=[("ss", 1)])
                    S.emit("tensor",
                           lambda e, c=c: e.matmul(P[5][0:1, :T], onec, src3[:, c, :],
                                                   start=(c == 0), stop=(c == C - 1)))
                    S.emit("tensor",
                           lambda e, c=c: e.matmul(P[7][0:1, :T], onec, sqb[c % 2][:, :T],
                                                   start=(c == 0), stop=(c == C - 1)),
                           waits=[("ss", ss_b + c + 1)], incs=[("st", 1)])
            wt = S.w("st")
            S.emit("vector", lambda e: e.tensor_scalar(rm[:, :T], P[5][0:1, :T], 1.0 / D, None, OP.mult),
                   waits=wt)
            S.emit("vector", lambda e: e.tensor_scalar(rowA[:, :T], P[7][0:1, :T], 1.0 / D, EPS,
                                                       OP.mult, OP.add))
            S.emit("vector", lambda e: e.tensor_tensor(rowB[:, :T], rm[:, :T], rm[:, :T], OP.mult))
            S.emit("vector", lambda e: e.tensor_tensor(rowA[:, :T], rowA[:, :T], rowB[:, :T], OP.subtract),
                   incs=[("sv", 1)])
            S.emit("scalar", lambda e: e.activation(rowB[:, :T], rowA[:, :T], AF.Sqrt),
                   waits=S.w("sv"), incs=[("ss", 1)])
            S.emit("vector", lambda e: e.reciprocal(rr[:, :T], rowB[:, :T]),
                   waits=S.w("ss"), incs=[("sv", 1)])
            wv_ = S.w("sv")
            S.emit("tensor", lambda e: e.matmul(P[2][:, :T], ones128, rm[:, :T],
                                                start=True, stop=True), waits=wv_)
            S.emit("tensor", lambda e: e.matmul(P[3][:, :T], ones128, rr[:, :T],
                                                start=True, stop=True), incs=[("st", 1)])

        def bcast_evac(T=TB):
            if NSPLIT >= C:
                return
            wt = S.w("st")
            S.emit("vector", lambda e: e.tensor_copy(mb_sb[:, :T], P[2][:, :T]),
                   waits=wt)
            S.emit("vector", lambda e: e.tensor_copy(rbx_sb[:, :T], P[3][:, :T]),
                   incs=[("sv", 1)])

        def ln_apply(src3, dst3, T=TB):
            bcast_evac(T)
            wt = S.w("st")
            wg = S.w("sv")
            for c in range(NSPLIT):
                S.emit("vector", lambda e, c=c: e.tensor_tensor(tsc[:, :T], src3[:, c, :], P[2][:, :T], OP.subtract),
                       waits=wt if c == 0 else [])
                S.emit("vector", lambda e, c=c: e.tensor_tensor(dst3[:, c, :], tsc[:, :T], P[3][:, :T], OP.mult),
                       incs=[("sv", 1)])
            for c in range(NSPLIT, C):
                S.emit("gpsimd", lambda e, c=c: e.tensor_tensor(tsc2[:, :T], src3[:, c, :], mb_sb[:, :T], OP.subtract),
                       waits=wg if c == NSPLIT else [])
                S.emit("gpsimd", lambda e, c=c: e.tensor_tensor(dst3[:, c, :], tsc2[:, :T], rbx_sb[:, :T], OP.mult),
                       incs=[("sg", 1)])

        def ln_residual(src3, T=TB):
            bcast_evac(T)
            wt = S.w("st")
            wg = S.w("sv")
            for c in range(NSPLIT):
                S.emit("vector", lambda e, c=c: e.tensor_tensor(tsc[:, :T], src3[:, c, :], P[2][:, :T], OP.subtract),
                       waits=wt if c == 0 else [])
                S.emit("vector", lambda e, c=c: e.tensor_tensor(tsc[:, :T], tsc[:, :T], P[3][:, :T], OP.mult))
                S.emit("vector", lambda e, c=c: e.tensor_tensor(XB[:, c, :T], XB[:, c, :T], tsc[:, :T], OP.add),
                       incs=[("sv", 1)])
            for c in range(NSPLIT, C):
                S.emit("gpsimd", lambda e, c=c: e.tensor_tensor(tsc2[:, :T], src3[:, c, :], mb_sb[:, :T], OP.subtract),
                       waits=wg if c == NSPLIT else [])
                S.emit("gpsimd", lambda e, c=c: e.tensor_tensor(tsc2[:, :T], tsc2[:, :T], rbx_sb[:, :T], OP.mult))
                S.emit("gpsimd", lambda e, c=c: e.tensor_tensor(XB[:, c, :T], XB[:, c, :T], tsc2[:, :T], OP.add),
                       incs=[("sg", 1)])

        S.block()
        a1_sv = []
        for t in range(NTILES):
            t0 = t * TB
            lx_b = S.ctr["lx"]
            for c in range(C):
                S.emit("sync", lambda e, c=c, t0=t0: e.dma_start(
                    out=XB[:, c, :], in_=xt3[:, c, t0:t0 + TB]),
                    waits=(S.w("sv", "sg") if c == 0 else []), incs=[("lx", 16)])
            wx = S.w("lx")
            S.emit("vector", lambda e: e.nop(), waits=wx)
            ln_stats(XB, True, chunk_gate=("lx", lx_b, 16))
            ln_apply(XB, h1a[:, :, t0:t0 + TB])
            a1_sv.append(S.ctr["sv"])

        wkb_free = 0
        wvb_free = 0
        for fb in range(4):
            S.emit("gpsimd", lambda e, fb=fb: e.dma_start(out=wkb, in_=wk3[:, :, fb * 512:(fb + 1) * 512]),
                   waits=S.w("st"), incs=[("lg", 16)])
            wlk = S.w("lg")
            S.emit("gpsimd", lambda e, fb=fb: e.dma_start(out=wvb, in_=wv3[:, :, fb * 512:(fb + 1) * 512]),
                   incs=[("lg", 16)])
            wlv = S.w("lg")
            ss_b = S.ctr["ss"]
            for g in range(C):
                pk = P[0] if g % 2 == 0 else P[1]
                first_w = []
                if g == 0:
                    first_w += wlk
                    if fb > 0:
                        first_w += [("ss", ss_b)]
                if g >= 2:
                    first_w += [("ss", ss_b + g - 1)]
                if fb == 0:
                    first_w += [("sv", a1_sv[g // 4])]
                for c in range(C):
                    S.emit("tensor", lambda e, c=c, g=g, pk=pk: e.matmul(
                        pk[:, :], h1a[:, c, fbg(fb, g)], wkb[:, c, :],
                        start=(c == 0), stop=(c == C - 1)),
                        waits=(first_w if c == 0 else []),
                        incs=([("st", 1)] if c == C - 1 else []))
                S.emit("scalar", lambda e, g=g, pk=pk: e.activation(kfb[:, g, :], pk[:, :], AF.Relu),
                       waits=S.w("st"), incs=[("ss", 1)])
            wkb_free = S.ctr["st"]
            sv_b = S.ctr["sv"]
            for g in range(C):
                first_w = []
                if g == 0:
                    first_w += wlv + [("sv", sv_b)]
                if g >= 1:
                    first_w += [("sv", sv_b + g)]
                for c in range(C):
                    S.emit("tensor", lambda e, c=c, g=g: e.matmul(
                        P[6][:, :], h1a[:, c, fbg(fb, g)], wvb[:, c, :],
                        start=(c == 0), stop=(c == C - 1)),
                        waits=(first_w if c == 0 else []),
                        incs=([("st", 1)] if c == C - 1 else []))
                S.emit("vector", lambda e, g=g: e.tensor_copy(vfb[:, g, :], P[6][:, :]),
                       waits=S.w("st"), incs=[("sv", 1)])
            wvb_free = S.ctr["st"]
            ev_all = [("ss", S.ctr["ss"]), ("sv", S.ctr["sv"])]
            for hp in range(4):
                for g in range(C):
                    S.emit("tensor", lambda e, g=g, hp=hp: e.matmul(
                        P[4][:, hp * 128:(hp + 1) * 128],
                        kfb[:, g, hp * 128:(hp + 1) * 128],
                        vfb[:, g, hp * 128:(hp + 1) * 128],
                        start=(g == 0), stop=(g == C - 1)),
                        waits=(ev_all if (hp == 0 and g == 0) else []))
                    S.emit("tensor", lambda e, g=g, hp=hp: e.matmul(
                        P[5][:, hp:hp + 1],
                        kfb[:, g, hp * 128:(hp + 1) * 128], onescol_bf,
                        start=(g == 0), stop=(g == C - 1)),
                        incs=([("st", 1)] if (hp == 3 and g == C - 1) else []))
            wt = S.w("st")
            for hp in range(4):
                col = (fb * 4 + hp) * 65
                S.emit("vector", lambda e, hp=hp, col=col: e.tensor_copy(
                    kvks_sb[0:64, col:col + 64], P[4][0:64, hp * 128:hp * 128 + 64]),
                    waits=(wt if hp == 0 else []))
                S.emit("vector", lambda e, hp=hp, col=col: e.tensor_copy(
                    kvks_sb[64:128, col:col + 64], P[4][64:128, hp * 128 + 64:hp * 128 + 128]))
                S.emit("vector", lambda e, hp=hp, col=col: e.tensor_copy(
                    kvks_sb[:, col + 64:col + 65], P[5][:, hp:hp + 1]),
                    incs=[("sv", 1)])

        S.block()
        S.emit("gpsimd", lambda e: e.dma_start(out=cc_in.ap(), in_=kvks_sb), waits=S.w("sv"),
               incs=[("lg", 16)])
        S.emit("gpsimd", lambda e: e.collective_compute(
            "AllReduce", OP.add, replica_groups=[list(range(NCORES))],
            ins=[cc_in.ap().opt()], outs=[cc_out.ap().opt()]),
            waits=S.w("lg"), incs=[("cc", 1)])
        S.emit("gpsimd", lambda e: e.dma_start(out=kvks_sb, in_=cc_out.ap()), waits=S.w("cc"),
               incs=[("lg", 16)])
        S.emit("vector", lambda e: e.tensor_copy(kvks_bf, kvks_sb), waits=S.w("lg"),
               incs=[("sv", 1)])

        for t in range(NTILES):
            t0 = t * TB
            S.block()
            lx_b = S.ctr["lx"]
            for c in range(C):
                S.emit("sync", lambda e, c=c, t0=t0: e.dma_start(
                    out=XB[:, c, :], in_=xt3[:, c, t0:t0 + TB]),
                    waits=([("so", 256 * t)] if c == 0 else []), incs=[("lx", 16)])
            wx = S.w("lx")
            S.emit("vector", lambda e: e.nop(), waits=wx)

            def wload(dst, raw_src, cache_src, waits):
                if t == 0:
                    S.emit("gpsimd", lambda e, dst=dst, s=raw_src: e.dma_start(out=dst, in_=s),
                           waits=waits, incs=[("lg", 16)])
                    dep = ("lg", S.ctr["lg"])
                    S.emit("sync", lambda e, dst=dst, s=cache_src: e.dma_start(out=s, in_=dst),
                           waits=[dep], incs=[("sc", 16)])
                    return dep, S.ctr["sc"]
                else:
                    S.emit("sync", lambda e, dst=dst, s=cache_src: e.dma_start(out=dst, in_=s),
                           waits=waits, incs=[("lx", 16)])
                    return ("lx", S.ctr["lx"]), 0

            tile_st0 = S.ctr["st"]
            sc_full = S.ctr["sc"]
            ln_stats(XB, True, chunk_gate=("lx", lx_b, 16))
            ln_apply(XB, hbuf)
            h1_done = S.w("sv", "sg")

            wq_free = []
            wq_sc = []
            ss_b = S.ctr["ss"]
            for fc in range(FC):
                fw = [("st", wq_free[fc - 2])] if fc >= 2 else [("st", tile_st0)]
                if t == 0 and fc >= 2:
                    fw.append(("sc", wq_sc[fc - 2]))
                if t == 1 and fc == 0:
                    fw.append(("sc", sc_full))
                wb = w1 if fc % 2 == 0 else w2
                dep, csc = wload(wb, wq_t[fc], acache[fc], fw)
                wq_sc.append(csc)
                pq = P[0] if fc % 2 == 0 else P[1]
                first_w = [dep]
                if fc == 0:
                    first_w += h1_done
                if fc >= 2:
                    first_w += [("ss", ss_b + fc - 1)]
                for c in range(C):
                    S.emit("tensor", lambda e, c=c, pq=pq, wb=wb: e.matmul(
                        pq[:, :], wb[:, c, :], hbuf[:, c, :],
                        start=(c == 0), stop=(c == C - 1)),
                        waits=(first_w if c == 0 else []),
                        incs=([("st", 1)] if c == C - 1 else []))
                wq_free.append(S.ctr["st"])
                S.emit("scalar", lambda e, fc=fc, pq=pq: e.activation(qo[:, fc, :], pq[:, :], AF.Relu),
                       waits=S.w("st"), incs=[("ss", 1)])

            wq_done = S.w("ss", "sv")
            for tc in range(4):
                for hp in range(NHP):
                    col0 = tc * 32 + 2 * hp
                    S.emit("tensor", lambda e, tc=tc, hp=hp, col0=col0: e.matmul(
                        P[4][:, col0:col0 + 1],
                        qo[0:64, hp, tc * 128:(tc + 1) * 128],
                        kvks_bf[0:64, hp * 65 + 64:hp * 65 + 65],
                        start=True, stop=True),
                        waits=(wq_done if (tc == 0 and hp == 0) else []))
                    S.emit("tensor", lambda e, tc=tc, hp=hp, col0=col0: e.matmul(
                        P[4][:, col0 + 1:col0 + 2],
                        qo[64:128, hp, tc * 128:(tc + 1) * 128],
                        kvks_bf[64:128, hp * 65 + 64:hp * 65 + 65],
                        start=True, stop=True, tile_position=(64, 0)),
                        incs=([("st", 1)] if (tc == 3 and hp == NHP - 1) else []))
            S.emit("vector", lambda e: e.tensor_scalar(rn_tok, P[4][:, 0:128], 1e-6, None, OP.max),
                   waits=S.w("st"))
            S.emit("vector", lambda e: e.reciprocal(rn_tok, rn_tok))
            for tc in range(4):
                for g in range(4):
                    S.emit("vector", lambda e, tc=tc, g=g: e.transpose(
                        rn_f[0:32, tc * 128 + g * 32:tc * 128 + (g + 1) * 32],
                        rn_tok[g * 32:(g + 1) * 32, tc * 32:(tc + 1) * 32]),
                        incs=([("sv", 1)] if (tc == 3 and g == 3) else []))

            rn_ready = S.w("sv")
            sv_b, ss_b = S.ctr["sv"], S.ctr["ss"]
            for hp in range(NHP):
                pq = P[0] if hp % 2 == 0 else P[1]
                pr = P[2] if hp % 2 == 0 else P[3]
                first_w = list(rn_ready) if hp == 0 else []
                if hp >= 2:
                    first_w += [("sv", sv_b + hp - 1), ("ss", ss_b + hp - 1)]
                S.emit("tensor", lambda e, hp=hp, pq=pq: e.matmul(
                    pq[0:64, :], kvks_bf[0:64, hp * 65:hp * 65 + 64], qo[0:64, hp, :],
                    start=True, stop=True), waits=first_w)
                S.emit("tensor", lambda e, hp=hp, pq=pq: e.matmul(
                    pq[64:128, :], kvks_bf[64:128, hp * 65:hp * 65 + 64], qo[64:128, hp, :],
                    start=True, stop=True, tile_position=(64, 64)))
                S.emit("tensor", lambda e, hp=hp, pr=pr: e.matmul(
                    pr[:, :], sel32[:, hp * 128:(hp + 1) * 128], rn_f,
                    start=True, stop=True),
                    incs=[("st", 1)])
                sw = [("sv", sv_b + hp - 1)] if hp >= 2 else []
                S.emit("scalar", lambda e, hp=hp, pr=pr: e.activation(rb_sb[hp % 2], pr[:, :], AF.Identity),
                       waits=S.w("st") + sw, incs=[("ss", 1)])
                we = S.w("ss")
                S.emit("vector", lambda e, hp=hp, pq=pq: e.tensor_tensor(
                    aiy[0:64, hp, :], pq[0:64, :], rb_sb[hp % 2][0:64, :], OP.mult), waits=we)
                S.emit("vector", lambda e, hp=hp, pq=pq: e.tensor_tensor(
                    aiy[64:128, hp, :], pq[64:128, :], rb_sb[hp % 2][64:128, :], OP.mult),
                    incs=[("sv", 1)])

            ai_done = S.w("sv")
            wo_free = []
            wo_sc = []
            ss_b = S.ctr["ss"]
            for fc in range(FC):
                fw = [("st", wq_free[-1])] if fc < 2 else [("st", wo_free[fc - 2])]
                if t == 0:
                    fw.append(("sc", wq_sc[-1] if fc < 2 else wo_sc[fc - 2]))
                wb = w1 if fc % 2 == 0 else w2
                dep, csc = wload(wb, wo_t[fc], acache[FC + fc], fw)
                wo_sc.append(csc)
                po = P[0] if fc % 2 == 0 else P[1]
                first_w = [dep]
                if fc == 0:
                    first_w += ai_done
                if fc >= 2:
                    first_w += [("ss", ss_b + fc - 1)]
                for c in range(C):
                    S.emit("tensor", lambda e, c=c, po=po, wb=wb: e.matmul(
                        po[:, :], wb[:, c, :], aiy[:, c, :],
                        start=(c == 0), stop=(c == C - 1)),
                        waits=(first_w if c == 0 else []),
                        incs=([("st", 1)] if c == C - 1 else []))
                wo_free.append(S.ctr["st"])
                S.emit("scalar", lambda e, fc=fc, po=po: e.activation(
                    qo[:, fc, :], po[:, :], AF.Identity, bias=bo_s[:, fc:fc + 1]),
                    waits=S.w("st"), incs=[("ss", 1)])

            S.emit("tensor", lambda e: e.nop(), waits=S.w("ss"))
            ln_stats(qo, False)
            resid1_sv0 = S.ctr["sv"]
            ln_residual(qo)
            resid1_w = S.w("sv", "sg")

            ln_stats(XB, True, chunk_gate=("sv", resid1_sv0, 1))
            ln_apply(XB, hbuf)

            h2_done = S.w("sv", "sg")
            for c in range(C):
                S.emit("tensor", lambda e, c=c: e.matmul(
                    P[5][0:1, :], onescol_bf, hbuf[:, c, :],
                    start=(c == 0), stop=(c == C - 1)),
                    waits=(h2_done if c == 0 else []),
                    incs=([("st", 1)] if c == C - 1 else []))
            S.emit("vector", lambda e: e.tensor_copy(h2s_row, P[5][0:1, :]), waits=S.w("st"),
                   incs=[("sv", 1)])

            h2s_ready = S.w("sv")
            gfree = []
            gu_sc = []
            ss_b, sv_b = S.ctr["ss"], S.ctr["sv"]
            for hc in range(HC):
                fw = [("st", gfree[hc - 2])] if hc >= 2 else [("st", tile_st0)]
                if t == 0 and hc >= 2:
                    fw.append(("sc", gu_sc[hc - 2]))
                wload(w3[hc % 2], g_t[hc], gcache[hc], fw)
                dep, _ = wload(w4[hc % 2], u_t[hc], ucache[hc], [])
                gu_sc.append(S.ctr["sc"])
                if hc == 7:
                    dw0_dep, dw0_csc = wload(dw[0], d_t[0], dcache[0],
                                             list(resid1_w))
                pg = P[0] if hc % 2 == 0 else P[2]
                pu = P[1] if hc % 2 == 0 else P[3]
                first_w = [dep]
                if hc == 0:
                    first_w += h2s_ready
                if hc >= 2:
                    first_w += [("ss", ss_b + hc - 1), ("sv", sv_b + hc - 1)]
                if hc >= 2:
                    S.emit("tensor", lambda e, hc=hc: e.matmul(
                        P[6][0:1, :], onescol_bf, gu[:, hc - 2, :],
                        start=(hc - 2 == 0), stop=False),
                        waits=[("sv", sv_b + hc - 1)])
                for c in range(C):
                    S.emit("tensor", lambda e, c=c, hc=hc, pg=pg: e.matmul(
                        pg[:, :], w3[hc % 2][:, c, :], hbuf[:, c, :],
                        start=(c == 0), stop=False),
                        waits=(first_w if c == 0 else []))
                S.emit("tensor", lambda e, pg=pg: e.matmul(
                    pg[:, :], negc_row, h2s_row, start=False, stop=True))
                for c in range(C):
                    S.emit("tensor", lambda e, c=c, hc=hc, pu=pu: e.matmul(
                        pu[:, :], w4[hc % 2][:, c, :], hbuf[:, c, :],
                        start=(c == 0), stop=False))
                S.emit("tensor", lambda e, pu=pu: e.matmul(
                    pu[:, :], negc_row, h2s_row, start=False, stop=True),
                    incs=[("st", 1)])
                gfree.append(S.ctr["st"])
                wt = S.w("st")
                S.emit("scalar", lambda e, hc=hc, pg=pg: e.activation(
                    g_tmp[hc % 2], pg[:, :], AF.Silu, scale=sg_s[:, hc:hc + 1]),
                    waits=wt, incs=[("ss", 1)])
                S.emit("vector", lambda e, hc=hc, pu=pu: e.scalar_tensor_tensor(
                    gu[:, hc, :], pu[:, :], su_s[:, hc:hc + 1], g_tmp[hc % 2],
                    OP.mult, OP.mult),
                    waits=S.w("ss"), incs=[("sv", 1)])
            for hc in (HC - 2, HC - 1):
                S.emit("tensor", lambda e, hc=hc: e.matmul(
                    P[6][0:1, :], onescol_bf, gu[:, hc, :],
                    start=False, stop=(hc == HC - 1)),
                    waits=[("sv", sv_b + hc + 1)],
                    incs=([("st", 1)] if hc == HC - 1 else []))
            S.emit("vector", lambda e: e.tensor_copy(gus_row, P[6][0:1, :]), waits=S.w("st"),
                   incs=[("sv", 1)])

            gus_ready = S.w("sv")
            dfree = []
            dn_sc = []
            ss_b = S.ctr["ss"]
            for oc in range(FC):
                if oc == 0:
                    dep, csc = dw0_dep, dw0_csc
                else:
                    if oc == 1:
                        fw = [("st", gfree[-1])]
                    else:
                        fw = [("st", dfree[oc - 2])]
                        if t == 0:
                            fw.append(("sc", dn_sc[oc - 2]))
                    dep, csc = wload(dw[oc % 2], d_t[oc], dcache[oc], fw)
                dn_sc.append(csc)
                pd = P[0] if oc % 2 == 0 else P[1]
                first_w = [dep]
                if oc == 0:
                    first_w += gus_ready
                if oc >= 2:
                    first_w += [("ss", ss_b + oc - 1)]
                for c in range(HC):
                    S.emit("tensor", lambda e, c=c, oc=oc, pd=pd: e.matmul(
                        pd[:, :], dw[oc % 2][:, c, :], gu[:, c, :],
                        start=(c == 0), stop=False),
                        waits=(first_w if c == 0 else []))
                S.emit("tensor", lambda e, pd=pd: e.matmul(
                    pd[:, :], negc_row, gus_row, start=False, stop=True),
                    incs=[("st", 1)])
                dfree.append(S.ctr["st"])
                S.emit("scalar", lambda e, oc=oc, pd=pd: e.activation(
                    aiy[:, oc, :], pd[:, :], AF.Identity, scale=sd_s[:, oc:oc + 1]),
                    waits=S.w("st"), incs=[("ss", 1)])

            S.emit("tensor", lambda e: e.nop(), waits=S.w("ss"))
            ln_stats(aiy, False)
            resid2_sv0 = S.ctr["sv"]
            ln_residual(aiy)
            for c in range(C):
                S.emit("sync", lambda e, c=c, t0=t0: e.dma_start(
                    out=out3[:, c, t0:t0 + TB], in_=XB[:, c, :]),
                    waits=[("sv", resid2_sv0 + c + 1)], incs=[("so", 16)])

        S.block()
        S.emit("sync", lambda e: e.nop(), waits=[("so", 256 * NTILES)])

        S.realize(nc, sems)
    return nc


def fbg(fb, g):
    return slice(g * 128, (g + 1) * 128)


_NC_CACHE = None


def _get_program():
    global _NC_CACHE
    if _NC_CACHE is None:
        _NC_CACHE = build_program()
    return _NC_CACHE


def kernel(L0_attn_hidden_states, L0_attn_Wq, L0_attn_Wk, L0_attn_Wv, L0_attn_Wo,
           L0_attn_bo, L0_ffn_wq_gate, L0_ffn_s_gate, L0_ffn_wq_up, L0_ffn_s_up,
           L0_ffn_wq_down, L0_ffn_s_down):
    x = np.asarray(L0_attn_hidden_states)
    S_FULL = x.shape[1]
    xT = np.ascontiguousarray(x[0].T)

    def tile_proj(W):
        return np.ascontiguousarray(
            np.asarray(W, dtype=np.float32).reshape(C, 128, FC, 128).transpose(2, 1, 0, 3))

    wq_t = tile_proj(L0_attn_Wq)
    wo_t = tile_proj(L0_attn_Wo)
    wk = np.ascontiguousarray(np.asarray(L0_attn_Wk, dtype=np.float32))
    wv = np.ascontiguousarray(np.asarray(L0_attn_Wv, dtype=np.float32))

    def tile_quant(Wq_int, out_chunks):
        W = np.asarray(Wq_int, dtype=np.int32)
        O, I = W.shape
        WT = np.ascontiguousarray(W.T)
        return np.ascontiguousarray(
            WT.reshape(I // 128, 128, out_chunks, 128).transpose(2, 1, 0, 3))

    g_t = tile_quant(L0_ffn_wq_gate, HC)
    u_t = tile_quant(L0_ffn_wq_up, HC)
    d_t = tile_quant(L0_ffn_wq_down, FC)

    def col2(v, nch):
        return np.ascontiguousarray(np.asarray(v, dtype=np.float32).reshape(nch, 128).T)

    bo2 = col2(L0_attn_bo, C)
    sg2 = col2(L0_ffn_s_gate, HC)
    su2 = col2(L0_ffn_s_up, HC)
    sd2 = col2(L0_ffn_s_down, C)

    sel32 = np.zeros((32, NHP * 128), dtype=np.float32)
    for hp in range(NHP):
        sel32[2 * hp, hp * 128:hp * 128 + 64] = 1.0
        sel32[2 * hp + 1, hp * 128 + 64:(hp + 1) * 128] = 1.0

    nc = _get_program()
    shared = {"wq_t": wq_t, "wo_t": wo_t, "wk": wk, "wv": wv, "g_t": g_t,
              "u_t": u_t, "d_t": d_t, "bo2": bo2, "sg2": sg2, "su2": su2,
              "sd2": sd2, "sel32": sel32}
    in_maps = []
    for c in range(NCORES):
        m = dict(shared)
        m["xt"] = np.ascontiguousarray(xT[:, c * TOK:(c + 1) * TOK])
        in_maps.append(m)

    res = run_bass_kernel_spmd(nc, in_maps, core_ids=list(range(NCORES)))
    out = np.empty((D, S_FULL), dtype=np.float32)
    for c in range(NCORES):
        out[:, c * TOK:(c + 1) * TOK] = res.results[c]["out_t"]
    return np.ascontiguousarray(out.T).reshape(1, S_FULL, D)


if __name__ == "__main__":
    print("building program...")
    nc = build_program()
    n = sum(1 for _ in nc.all_instructions())
    print("instructions:", n)


# version 19
# speedup vs baseline: 1.0773x; 1.0326x over previous
import contextlib
import numpy as np
import concourse.bass as bass
import concourse.mybir as mybir
from concourse.bass_utils import run_bass_kernel_spmd

F32 = mybir.dt.float32
BF16 = mybir.dt.bfloat16
I32 = mybir.dt.int32
F32R = mybir.dt.float32r
AF = mybir.ActivationFunctionType
OP = mybir.AluOpType

NCORES = 8
TOK = 2048
TB = 512
NTILES = TOK // TB
C = 16
D = 2048
HID = 5632
HC = 44
FC = 16
NHP = 16
EPS = 1e-5

ENGINES = ("sync", "gpsimd", "vector", "scalar", "tensor")
SEMS = ("lx", "lg", "sv", "ss", "st", "so", "cc", "sc", "sg")
NSPLIT = 16


class Sched:
    def __init__(self):
        self.blocks = []
        self.cur = None
        self.ctr = {s: 0 for s in SEMS}
        self.block()

    def block(self):
        self.cur = {}
        self.blocks.append(self.cur)

    def emit(self, eng, fn, waits=(), incs=()):
        for s, n in incs:
            self.ctr[s] += n
        self.cur.setdefault(eng, []).append((tuple(waits), fn, tuple(incs)))

    def w(self, *sems):
        return [(s, self.ctr[s]) for s in sems if self.ctr[s] > 0]

    def realize(self, nc, sems):
        for blk in self.blocks:
            if not blk:
                continue
            with nc.Block() as B:
                for eng_name, items in blk.items():
                    def body(eng, items=items):
                        for waits, fn, incs in items:
                            for s, v in waits:
                                if v > 0:
                                    eng.wait_ge(sems[s], v)
                            inst = fn(eng)
                            for s, n in incs:
                                inst.then_inc(sems[s], n)
                    getattr(B, eng_name)(body)


def build_program():
    nc = bass.Bass("TRN2", target_bir_lowering=False, detect_race_conditions=False)

    xt = nc.dram_tensor("xt", [D, TOK], F32, kind="ExternalInput").ap()
    wq_t = nc.dram_tensor("wq_t", [FC, 128, C, 128], F32, kind="ExternalInput").ap()
    wo_t = nc.dram_tensor("wo_t", [FC, 128, C, 128], F32, kind="ExternalInput").ap()
    wk = nc.dram_tensor("wk", [D, D], F32, kind="ExternalInput").ap()
    wv = nc.dram_tensor("wv", [D, D], F32, kind="ExternalInput").ap()
    g_t = nc.dram_tensor("g_t", [HC, 128, C, 128], I32, kind="ExternalInput").ap()
    u_t = nc.dram_tensor("u_t", [HC, 128, C, 128], I32, kind="ExternalInput").ap()
    d_t = nc.dram_tensor("d_t", [FC, 128, HC, 128], I32, kind="ExternalInput").ap()
    bo2 = nc.dram_tensor("bo2", [128, C], F32, kind="ExternalInput").ap()
    sg2 = nc.dram_tensor("sg2", [128, HC], F32, kind="ExternalInput").ap()
    su2 = nc.dram_tensor("su2", [128, HC], F32, kind="ExternalInput").ap()
    sd2 = nc.dram_tensor("sd2", [128, C], F32, kind="ExternalInput").ap()
    sel32_d = nc.dram_tensor("sel32", [32, NHP * 128], F32, kind="ExternalInput").ap()
    out_t = nc.dram_tensor("out_t", [D, TOK], F32, kind="ExternalOutput").ap()
    cc_in = nc.dram_tensor("cc_in", [128, NHP * 65], F32)
    cc_out = nc.dram_tensor("cc_out", [128, NHP * 65], F32)
    gcache = nc.dram_tensor("gcache", [HC, 128, C * 128], BF16).ap()
    ucache = nc.dram_tensor("ucache", [HC, 128, C * 128], BF16).ap()
    dcache = nc.dram_tensor("dcache", [FC, 128, HC * 128], BF16).ap()
    acache = nc.dram_tensor("acache", [2 * FC, 128, C * 128], BF16).ap()

    xt3 = xt.rearrange("(c p) t -> p c t", p=128)
    out3 = out_t.rearrange("(c p) t -> p c t", p=128)
    wk3 = wk.rearrange("(c p) f -> p c f", p=128)
    wv3 = wv.rearrange("(c p) f -> p c f", p=128)

    S = Sched()

    with contextlib.ExitStack() as ctx:
        E = ctx.enter_context
        ABF = E(nc.sbuf_tensor("ABF", [128, 70848], BF16)).ap()
        AF_ = E(nc.sbuf_tensor("AF", [128, 13472], F32)).ap()
        sel32 = E(nc.sbuf_tensor("sel32_s", [32, NHP * 128], F32)).ap()
        ones128 = E(nc.sbuf_tensor("ones128", [1, 128], F32)).ap()
        rn_f = E(nc.sbuf_tensor("rn_f", [32, TB], F32)).ap()
        bo_s = E(nc.sbuf_tensor("bo_s", [128, C], F32)).ap()
        sg_s = E(nc.sbuf_tensor("sg_s", [128, HC], F32)).ap()
        su_s = E(nc.sbuf_tensor("su_s", [128, HC], F32)).ap()
        sd_s = E(nc.sbuf_tensor("sd_s", [128, C], F32)).ap()
        P = [E(nc.psum_tensor(f"P{i}", [128, 512], F32)).ap() for i in range(8)]
        sems = {s: E(nc.semaphore(s)) for s in SEMS}

        def bfv(off, n):
            return ABF[:, off:off + n]

        kvks_bf = bfv(0, 1040)
        xbs = [bfv(1056, 512), bfv(1568, 512)]
        sqb = [bfv(2080, 512), bfv(2592, 512)]
        g_tmp = [bfv(3104, 512), bfv(3616, 512)]
        h2s_row = ABF[0:1, 4128:4640]
        gus_row = ABF[0:1, 4640:5152]
        negc_row = ABF[0:1, 5152:5280]
        onescol_bf = ABF[:, 5280:5281]
        RB = 5312
        h1a = bfv(RB, C * TOK).rearrange("p (c t) -> p c t", t=TOK)
        kfb = bfv(RB + 32768, C * 512).rearrange("p (g f) -> p g f", f=512)
        vfb = bfv(RB + 40960, C * 512).rearrange("p (g f) -> p g f", f=512)
        wkb = bfv(RB + 49152, C * 512).rearrange("p (c f) -> p c f", f=512)
        wvb = bfv(RB + 57344, C * 512).rearrange("p (c f) -> p c f", f=512)
        hbuf = bfv(RB, C * TB).rearrange("p (c t) -> p c t", t=TB)
        qo = bfv(RB + 8192, C * TB).rearrange("p (c t) -> p c t", t=TB)
        aiy = bfv(RB + 16384, C * TB).rearrange("p (c t) -> p c t", t=TB)
        gu = bfv(RB + 24576, HC * TB).rearrange("p (c t) -> p c t", t=TB)
        w1 = bfv(RB + 47104, C * 128).rearrange("p (c f) -> p c f", f=128)
        w2 = bfv(RB + 49152, C * 128).rearrange("p (c f) -> p c f", f=128)
        w3 = [bfv(RB + 51200, C * 128).rearrange("p (c f) -> p c f", f=128),
              bfv(RB + 53248, C * 128).rearrange("p (c f) -> p c f", f=128)]
        w4 = [bfv(RB + 55296, C * 128).rearrange("p (c f) -> p c f", f=128),
              bfv(RB + 57344, C * 128).rearrange("p (c f) -> p c f", f=128)]
        dw = [bfv(RB + 8192, HC * 128).rearrange("p (c f) -> p c f", f=128),
              bfv(RB, HC * 128).rearrange("p (c f) -> p c f", f=128)]

        def fv(off, n):
            return AF_[:, off:off + n]

        kvks_sb = fv(0, 1040)
        rm = AF_[0:1, 1040:1552]
        rr = AF_[0:1, 1552:2064]
        rowA = AF_[0:1, 2064:2576]
        rowB = AF_[0:1, 2576:3088]
        rb_sb = [fv(3088, 512), fv(3600, 512)]
        onescol_f32 = AF_[:, 13440:13441]
        rn_tok = fv(4112, 128)
        tsc = fv(4240, 512)
        XB = fv(4768, C * 512).rearrange("p (c t) -> p c t", t=512)
        tsc2 = fv(12960, 512)
        mb_sb = kvks_sb[:, 0:512]
        rbx_sb = kvks_sb[:, 512:1024]

        S.emit("sync", lambda e: e.dma_start(out=sel32, in_=sel32_d), incs=[("lx", 16)])
        S.emit("sync", lambda e: e.dma_start(out=bo_s, in_=bo2), incs=[("lx", 16)])
        S.emit("sync", lambda e: e.dma_start(out=sg_s, in_=sg2), incs=[("lx", 16)])
        S.emit("sync", lambda e: e.dma_start(out=su_s, in_=su2), incs=[("lx", 16)])
        S.emit("sync", lambda e: e.dma_start(out=sd_s, in_=sd2), incs=[("lx", 16)])
        S.emit("vector", lambda e: e.memset(ones128, 1.0), incs=[("sv", 1)])
        S.emit("vector", lambda e: e.memset(negc_row, -128.0), incs=[("sv", 1)])
        S.emit("vector", lambda e: e.memset(onescol_bf, 1.0), incs=[("sv", 1)])
        S.emit("vector", lambda e: e.memset(onescol_f32, 1.0), incs=[("sv", 1)])

        def ln_stats(src3, src_is_f32, T=TB, src_w=(), chunk_gate=None):
            onec = onescol_bf
            st_b = S.ctr["st"]
            if src_is_f32:
                ss_b = S.ctr["ss"]
                for c in range(C):
                    scr_w = [("st", st_b + c - 1)] if c >= 2 else []
                    ex = list(src_w) if c == 0 else []
                    if chunk_gate is not None:
                        ex = ex + [(chunk_gate[0], chunk_gate[1] + chunk_gate[2] * (c + 1))]
                    S.emit("scalar",
                           lambda e, c=c: e.activation(sqb[c % 2][:, :T], src3[:, c, :], AF.Square),
                           waits=scr_w + ex, incs=[("ss", 1)])
                    S.emit("tensor",
                           lambda e, c=c: e.matmul(P[5][0:1, :T], onescol_f32, src3[:, c, :],
                                                   start=(c == 0), stop=(c == C - 1)),
                           waits=ex if c == 0 else (ex if chunk_gate is not None else []))
                    S.emit("tensor",
                           lambda e, c=c: e.matmul(P[7][0:1, :T], onec, sqb[c % 2][:, :T],
                                                   start=(c == 0), stop=(c == C - 1)),
                           waits=[("ss", ss_b + c + 1)], incs=[("st", 1)])
            else:
                ss_b = S.ctr["ss"]
                for c in range(C):
                    scr_w = [("st", st_b + c - 1)] if c >= 2 else []
                    S.emit("scalar",
                           lambda e, c=c: e.activation(sqb[c % 2][:, :T], src3[:, c, :], AF.Square),
                           waits=scr_w, incs=[("ss", 1)])
                    S.emit("tensor",
                           lambda e, c=c: e.matmul(P[5][0:1, :T], onec, src3[:, c, :],
                                                   start=(c == 0), stop=(c == C - 1)))
                    S.emit("tensor",
                           lambda e, c=c: e.matmul(P[7][0:1, :T], onec, sqb[c % 2][:, :T],
                                                   start=(c == 0), stop=(c == C - 1)),
                           waits=[("ss", ss_b + c + 1)], incs=[("st", 1)])
            wt = S.w("st")
            S.emit("vector", lambda e: e.tensor_scalar(rm[:, :T], P[5][0:1, :T], 1.0 / D, None, OP.mult),
                   waits=wt)
            S.emit("vector", lambda e: e.tensor_scalar(rowA[:, :T], P[7][0:1, :T], 1.0 / D, EPS,
                                                       OP.mult, OP.add))
            S.emit("vector", lambda e: e.tensor_tensor(rowB[:, :T], rm[:, :T], rm[:, :T], OP.mult))
            S.emit("vector", lambda e: e.tensor_tensor(rowA[:, :T], rowA[:, :T], rowB[:, :T], OP.subtract),
                   incs=[("sv", 1)])
            S.emit("scalar", lambda e: e.activation(rowB[:, :T], rowA[:, :T], AF.Sqrt),
                   waits=S.w("sv"), incs=[("ss", 1)])
            S.emit("vector", lambda e: e.reciprocal(rr[:, :T], rowB[:, :T]),
                   waits=S.w("ss"), incs=[("sv", 1)])
            wv_ = S.w("sv")
            S.emit("tensor", lambda e: e.matmul(P[2][:, :T], ones128, rm[:, :T],
                                                start=True, stop=True), waits=wv_)
            S.emit("tensor", lambda e: e.matmul(P[3][:, :T], ones128, rr[:, :T],
                                                start=True, stop=True), incs=[("st", 1)])

        def bcast_evac(T=TB):
            if NSPLIT >= C:
                return
            wt = S.w("st")
            S.emit("vector", lambda e: e.tensor_copy(mb_sb[:, :T], P[2][:, :T]),
                   waits=wt)
            S.emit("vector", lambda e: e.tensor_copy(rbx_sb[:, :T], P[3][:, :T]),
                   incs=[("sv", 1)])

        def ln_apply(src3, dst3, T=TB):
            bcast_evac(T)
            wt = S.w("st")
            wg = S.w("sv")
            for c in range(NSPLIT):
                S.emit("vector", lambda e, c=c: e.tensor_tensor(tsc[:, :T], src3[:, c, :], P[2][:, :T], OP.subtract),
                       waits=wt if c == 0 else [])
                S.emit("vector", lambda e, c=c: e.tensor_tensor(dst3[:, c, :], tsc[:, :T], P[3][:, :T], OP.mult),
                       incs=[("sv", 1)])
            for c in range(NSPLIT, C):
                S.emit("gpsimd", lambda e, c=c: e.tensor_tensor(tsc2[:, :T], src3[:, c, :], mb_sb[:, :T], OP.subtract),
                       waits=wg if c == NSPLIT else [])
                S.emit("gpsimd", lambda e, c=c: e.tensor_tensor(dst3[:, c, :], tsc2[:, :T], rbx_sb[:, :T], OP.mult),
                       incs=[("sg", 1)])

        def ln_residual(src3, T=TB):
            bcast_evac(T)
            wt = S.w("st")
            wg = S.w("sv")
            for c in range(NSPLIT):
                S.emit("vector", lambda e, c=c: e.tensor_tensor(tsc[:, :T], src3[:, c, :], P[2][:, :T], OP.subtract),
                       waits=wt if c == 0 else [])
                S.emit("vector", lambda e, c=c: e.tensor_tensor(tsc[:, :T], tsc[:, :T], P[3][:, :T], OP.mult))
                S.emit("vector", lambda e, c=c: e.tensor_tensor(XB[:, c, :T], XB[:, c, :T], tsc[:, :T], OP.add),
                       incs=[("sv", 1)])
            for c in range(NSPLIT, C):
                S.emit("gpsimd", lambda e, c=c: e.tensor_tensor(tsc2[:, :T], src3[:, c, :], mb_sb[:, :T], OP.subtract),
                       waits=wg if c == NSPLIT else [])
                S.emit("gpsimd", lambda e, c=c: e.tensor_tensor(tsc2[:, :T], tsc2[:, :T], rbx_sb[:, :T], OP.mult))
                S.emit("gpsimd", lambda e, c=c: e.tensor_tensor(XB[:, c, :T], XB[:, c, :T], tsc2[:, :T], OP.add),
                       incs=[("sg", 1)])

        S.block()
        a1_sv = []
        for t in range(NTILES):
            t0 = t * TB
            lx_b = S.ctr["lx"]
            for c in range(C):
                S.emit("sync", lambda e, c=c, t0=t0: e.dma_start(
                    out=XB[:, c, :], in_=xt3[:, c, t0:t0 + TB]),
                    waits=(S.w("sv", "sg") if c == 0 else []), incs=[("lx", 16)])
            wx = S.w("lx")
            S.emit("vector", lambda e: e.nop(), waits=wx)
            ln_stats(XB, True, chunk_gate=("lx", lx_b, 16))
            ln_apply(XB, h1a[:, :, t0:t0 + TB])
            a1_sv.append(S.ctr["sv"])

        wkb_free = 0
        wvb_free = 0
        for fb in range(4):
            S.emit("gpsimd", lambda e, fb=fb: e.dma_start(out=wkb, in_=wk3[:, :, fb * 512:(fb + 1) * 512]),
                   waits=S.w("st"), incs=[("lg", 16)])
            wlk = S.w("lg")
            S.emit("gpsimd", lambda e, fb=fb: e.dma_start(out=wvb, in_=wv3[:, :, fb * 512:(fb + 1) * 512]),
                   incs=[("lg", 16)])
            wlv = S.w("lg")
            ss_b = S.ctr["ss"]
            for g in range(C):
                pk = P[0] if g % 2 == 0 else P[1]
                first_w = []
                if g == 0:
                    first_w += wlk
                    if fb > 0:
                        first_w += [("ss", ss_b)]
                if g >= 2:
                    first_w += [("ss", ss_b + g - 1)]
                if fb == 0:
                    first_w += [("sv", a1_sv[g // 4])]
                for c in range(C):
                    S.emit("tensor", lambda e, c=c, g=g, pk=pk: e.matmul(
                        pk[:, :], h1a[:, c, fbg(fb, g)], wkb[:, c, :],
                        start=(c == 0), stop=(c == C - 1)),
                        waits=(first_w if c == 0 else []),
                        incs=([("st", 1)] if c == C - 1 else []))
                S.emit("scalar", lambda e, g=g, pk=pk: e.activation(kfb[:, g, :], pk[:, :], AF.Relu),
                       waits=S.w("st"), incs=[("ss", 1)])
            wkb_free = S.ctr["st"]
            sv_b = S.ctr["sv"]
            for g in range(C):
                first_w = []
                if g == 0:
                    first_w += wlv + [("sv", sv_b)]
                if g >= 1:
                    first_w += [("sv", sv_b + g)]
                for c in range(C):
                    S.emit("tensor", lambda e, c=c, g=g: e.matmul(
                        P[6][:, :], h1a[:, c, fbg(fb, g)], wvb[:, c, :],
                        start=(c == 0), stop=(c == C - 1)),
                        waits=(first_w if c == 0 else []),
                        incs=([("st", 1)] if c == C - 1 else []))
                S.emit("vector", lambda e, g=g: e.tensor_copy(vfb[:, g, :], P[6][:, :]),
                       waits=S.w("st"), incs=[("sv", 1)])
            wvb_free = S.ctr["st"]
            ev_all = [("ss", S.ctr["ss"]), ("sv", S.ctr["sv"])]
            for hp in range(4):
                for g in range(C):
                    S.emit("tensor", lambda e, g=g, hp=hp: e.matmul(
                        P[4][:, hp * 128:(hp + 1) * 128],
                        kfb[:, g, hp * 128:(hp + 1) * 128],
                        vfb[:, g, hp * 128:(hp + 1) * 128],
                        start=(g == 0), stop=(g == C - 1)),
                        waits=(ev_all if (hp == 0 and g == 0) else []))
                    S.emit("tensor", lambda e, g=g, hp=hp: e.matmul(
                        P[5][:, hp:hp + 1],
                        kfb[:, g, hp * 128:(hp + 1) * 128], onescol_bf,
                        start=(g == 0), stop=(g == C - 1)),
                        incs=([("st", 1)] if (hp == 3 and g == C - 1) else []))
            wt = S.w("st")
            for hp in range(4):
                col = (fb * 4 + hp) * 65
                S.emit("vector", lambda e, hp=hp, col=col: e.tensor_copy(
                    kvks_sb[0:64, col:col + 64], P[4][0:64, hp * 128:hp * 128 + 64]),
                    waits=(wt if hp == 0 else []))
                S.emit("vector", lambda e, hp=hp, col=col: e.tensor_copy(
                    kvks_sb[64:128, col:col + 64], P[4][64:128, hp * 128 + 64:hp * 128 + 128]))
                S.emit("vector", lambda e, hp=hp, col=col: e.tensor_copy(
                    kvks_sb[:, col + 64:col + 65], P[5][:, hp:hp + 1]),
                    incs=[("sv", 1)])

        for t in range(NTILES):
            t0 = t * TB
            S.block()
            lx_b = S.ctr["lx"]
            for c in range(C):
                S.emit("sync", lambda e, c=c, t0=t0: e.dma_start(
                    out=XB[:, c, :], in_=xt3[:, c, t0:t0 + TB]),
                    waits=([("so", 256 * t)] if c == 0 else []), incs=[("lx", 16)])
            wx = S.w("lx")
            S.emit("vector", lambda e: e.nop(), waits=wx)
            if t == 0:
                S.emit("gpsimd", lambda e: e.dma_start(out=cc_in.ap(), in_=kvks_sb),
                       waits=S.w("sv"), incs=[("lg", 16)])
                S.emit("gpsimd", lambda e: e.collective_compute(
                    "AllReduce", OP.add, replica_groups=[list(range(NCORES))],
                    ins=[cc_in.ap().opt()], outs=[cc_out.ap().opt()]),
                    waits=S.w("lg"), incs=[("cc", 1)])

            def wload(dst, raw_src, cache_src, waits):
                if t == 0:
                    S.emit("gpsimd", lambda e, dst=dst, s=raw_src: e.dma_start(out=dst, in_=s),
                           waits=waits, incs=[("lg", 16)])
                    dep = ("lg", S.ctr["lg"])
                    S.emit("sync", lambda e, dst=dst, s=cache_src: e.dma_start(out=s, in_=dst),
                           waits=[dep], incs=[("sc", 16)])
                    return dep, S.ctr["sc"]
                else:
                    S.emit("sync", lambda e, dst=dst, s=cache_src: e.dma_start(out=dst, in_=s),
                           waits=waits, incs=[("lx", 16)])
                    return ("lx", S.ctr["lx"]), 0

            tile_st0 = S.ctr["st"]
            sc_full = S.ctr["sc"]
            ln_stats(XB, True, chunk_gate=("lx", lx_b, 16))
            ln_apply(XB, hbuf)
            h1_done = S.w("sv", "sg")

            wq_free = []
            wq_sc = []
            ss_b = S.ctr["ss"]
            for fc in range(FC):
                fw = [("st", wq_free[fc - 2])] if fc >= 2 else [("st", tile_st0)]
                if t == 0 and fc >= 2:
                    fw.append(("sc", wq_sc[fc - 2]))
                if t == 1 and fc == 0:
                    fw.append(("sc", sc_full))
                wb = w1 if fc % 2 == 0 else w2
                dep, csc = wload(wb, wq_t[fc], acache[fc], fw)
                wq_sc.append(csc)
                pq = P[0] if fc % 2 == 0 else P[1]
                first_w = [dep]
                if fc == 0:
                    first_w += h1_done
                if fc >= 2:
                    first_w += [("ss", ss_b + fc - 1)]
                for c in range(C):
                    S.emit("tensor", lambda e, c=c, pq=pq, wb=wb: e.matmul(
                        pq[:, :], wb[:, c, :], hbuf[:, c, :],
                        start=(c == 0), stop=(c == C - 1)),
                        waits=(first_w if c == 0 else []),
                        incs=([("st", 1)] if c == C - 1 else []))
                wq_free.append(S.ctr["st"])
                S.emit("scalar", lambda e, fc=fc, pq=pq: e.activation(qo[:, fc, :], pq[:, :], AF.Relu),
                       waits=S.w("st"), incs=[("ss", 1)])

            if t == 0:
                S.emit("scalar", lambda e: e.dma_start(out=kvks_sb, in_=cc_out.ap()),
                       waits=[("cc", 1)], incs=[("lx", 16)])
                S.emit("vector", lambda e: e.tensor_copy(kvks_bf, kvks_sb),
                       waits=[("lx", S.ctr["lx"])], incs=[("sv", 1)])

            wq_done = S.w("ss", "sv")
            for tc in range(4):
                for hp in range(NHP):
                    col0 = tc * 32 + 2 * hp
                    S.emit("tensor", lambda e, tc=tc, hp=hp, col0=col0: e.matmul(
                        P[4][:, col0:col0 + 1],
                        qo[0:64, hp, tc * 128:(tc + 1) * 128],
                        kvks_bf[0:64, hp * 65 + 64:hp * 65 + 65],
                        start=True, stop=True),
                        waits=(wq_done if (tc == 0 and hp == 0) else []))
                    S.emit("tensor", lambda e, tc=tc, hp=hp, col0=col0: e.matmul(
                        P[4][:, col0 + 1:col0 + 2],
                        qo[64:128, hp, tc * 128:(tc + 1) * 128],
                        kvks_bf[64:128, hp * 65 + 64:hp * 65 + 65],
                        start=True, stop=True, tile_position=(64, 0)),
                        incs=([("st", 1)] if (tc == 3 and hp == NHP - 1) else []))
            S.emit("vector", lambda e: e.tensor_scalar(rn_tok, P[4][:, 0:128], 1e-6, None, OP.max),
                   waits=S.w("st"))
            S.emit("vector", lambda e: e.reciprocal(rn_tok, rn_tok))
            for tc in range(4):
                for g in range(4):
                    S.emit("vector", lambda e, tc=tc, g=g: e.transpose(
                        rn_f[0:32, tc * 128 + g * 32:tc * 128 + (g + 1) * 32],
                        rn_tok[g * 32:(g + 1) * 32, tc * 32:(tc + 1) * 32]),
                        incs=([("sv", 1)] if (tc == 3 and g == 3) else []))

            rn_ready = S.w("sv")
            sv_b, ss_b = S.ctr["sv"], S.ctr["ss"]
            for hp in range(NHP):
                pq = P[0] if hp % 2 == 0 else P[1]
                pr = P[2] if hp % 2 == 0 else P[3]
                first_w = list(rn_ready) if hp == 0 else []
                if hp >= 2:
                    first_w += [("sv", sv_b + hp - 1), ("ss", ss_b + hp - 1)]
                S.emit("tensor", lambda e, hp=hp, pq=pq: e.matmul(
                    pq[0:64, :], kvks_bf[0:64, hp * 65:hp * 65 + 64], qo[0:64, hp, :],
                    start=True, stop=True), waits=first_w)
                S.emit("tensor", lambda e, hp=hp, pq=pq: e.matmul(
                    pq[64:128, :], kvks_bf[64:128, hp * 65:hp * 65 + 64], qo[64:128, hp, :],
                    start=True, stop=True, tile_position=(64, 64)))
                S.emit("tensor", lambda e, hp=hp, pr=pr: e.matmul(
                    pr[:, :], sel32[:, hp * 128:(hp + 1) * 128], rn_f,
                    start=True, stop=True),
                    incs=[("st", 1)])
                sw = [("sv", sv_b + hp - 1)] if hp >= 2 else []
                S.emit("scalar", lambda e, hp=hp, pr=pr: e.activation(rb_sb[hp % 2], pr[:, :], AF.Identity),
                       waits=S.w("st") + sw, incs=[("ss", 1)])
                we = S.w("ss")
                S.emit("vector", lambda e, hp=hp, pq=pq: e.tensor_tensor(
                    aiy[0:64, hp, :], pq[0:64, :], rb_sb[hp % 2][0:64, :], OP.mult), waits=we)
                S.emit("vector", lambda e, hp=hp, pq=pq: e.tensor_tensor(
                    aiy[64:128, hp, :], pq[64:128, :], rb_sb[hp % 2][64:128, :], OP.mult),
                    incs=[("sv", 1)])

            ai_done = S.w("sv")
            wo_free = []
            wo_sc = []
            ss_b = S.ctr["ss"]
            for fc in range(FC):
                fw = [("st", wq_free[-1])] if fc < 2 else [("st", wo_free[fc - 2])]
                if t == 0:
                    fw.append(("sc", wq_sc[-1] if fc < 2 else wo_sc[fc - 2]))
                wb = w1 if fc % 2 == 0 else w2
                dep, csc = wload(wb, wo_t[fc], acache[FC + fc], fw)
                wo_sc.append(csc)
                po = P[0] if fc % 2 == 0 else P[1]
                first_w = [dep]
                if fc == 0:
                    first_w += ai_done
                if fc >= 2:
                    first_w += [("ss", ss_b + fc - 1)]
                for c in range(C):
                    S.emit("tensor", lambda e, c=c, po=po, wb=wb: e.matmul(
                        po[:, :], wb[:, c, :], aiy[:, c, :],
                        start=(c == 0), stop=(c == C - 1)),
                        waits=(first_w if c == 0 else []),
                        incs=([("st", 1)] if c == C - 1 else []))
                wo_free.append(S.ctr["st"])
                S.emit("scalar", lambda e, fc=fc, po=po: e.activation(
                    qo[:, fc, :], po[:, :], AF.Identity, bias=bo_s[:, fc:fc + 1]),
                    waits=S.w("st"), incs=[("ss", 1)])

            S.emit("tensor", lambda e: e.nop(), waits=S.w("ss"))
            ln_stats(qo, False)
            resid1_sv0 = S.ctr["sv"]
            ln_residual(qo)
            resid1_w = S.w("sv", "sg")

            ln_stats(XB, True, chunk_gate=("sv", resid1_sv0, 1))
            ln_apply(XB, hbuf)

            h2_done = S.w("sv", "sg")
            for c in range(C):
                S.emit("tensor", lambda e, c=c: e.matmul(
                    P[5][0:1, :], onescol_bf, hbuf[:, c, :],
                    start=(c == 0), stop=(c == C - 1)),
                    waits=(h2_done if c == 0 else []),
                    incs=([("st", 1)] if c == C - 1 else []))
            S.emit("vector", lambda e: e.tensor_copy(h2s_row, P[5][0:1, :]), waits=S.w("st"),
                   incs=[("sv", 1)])

            h2s_ready = S.w("sv")
            gfree = []
            gu_sc = []
            ss_b, sv_b = S.ctr["ss"], S.ctr["sv"]
            for hc in range(HC):
                fw = [("st", gfree[hc - 2])] if hc >= 2 else [("st", tile_st0)]
                if t == 0 and hc >= 2:
                    fw.append(("sc", gu_sc[hc - 2]))
                wload(w3[hc % 2], g_t[hc], gcache[hc], fw)
                dep, _ = wload(w4[hc % 2], u_t[hc], ucache[hc], [])
                gu_sc.append(S.ctr["sc"])
                if hc == 7:
                    dw0_dep, dw0_csc = wload(dw[0], d_t[0], dcache[0],
                                             list(resid1_w))
                pg = P[0] if hc % 2 == 0 else P[2]
                pu = P[1] if hc % 2 == 0 else P[3]
                first_w = [dep]
                if hc == 0:
                    first_w += h2s_ready
                if hc >= 2:
                    first_w += [("ss", ss_b + hc - 1), ("sv", sv_b + hc - 1)]
                if hc >= 2:
                    S.emit("tensor", lambda e, hc=hc: e.matmul(
                        P[6][0:1, :], onescol_bf, gu[:, hc - 2, :],
                        start=(hc - 2 == 0), stop=False),
                        waits=[("sv", sv_b + hc - 1)])
                for c in range(C):
                    S.emit("tensor", lambda e, c=c, hc=hc, pg=pg: e.matmul(
                        pg[:, :], w3[hc % 2][:, c, :], hbuf[:, c, :],
                        start=(c == 0), stop=False),
                        waits=(first_w if c == 0 else []))
                S.emit("tensor", lambda e, pg=pg: e.matmul(
                    pg[:, :], negc_row, h2s_row, start=False, stop=True))
                for c in range(C):
                    S.emit("tensor", lambda e, c=c, hc=hc, pu=pu: e.matmul(
                        pu[:, :], w4[hc % 2][:, c, :], hbuf[:, c, :],
                        start=(c == 0), stop=False))
                S.emit("tensor", lambda e, pu=pu: e.matmul(
                    pu[:, :], negc_row, h2s_row, start=False, stop=True),
                    incs=[("st", 1)])
                gfree.append(S.ctr["st"])
                wt = S.w("st")
                S.emit("scalar", lambda e, hc=hc, pg=pg: e.activation(
                    g_tmp[hc % 2], pg[:, :], AF.Silu, scale=sg_s[:, hc:hc + 1]),
                    waits=wt, incs=[("ss", 1)])
                S.emit("vector", lambda e, hc=hc, pu=pu: e.scalar_tensor_tensor(
                    gu[:, hc, :], pu[:, :], su_s[:, hc:hc + 1], g_tmp[hc % 2],
                    OP.mult, OP.mult),
                    waits=S.w("ss"), incs=[("sv", 1)])
            for hc in (HC - 2, HC - 1):
                S.emit("tensor", lambda e, hc=hc: e.matmul(
                    P[6][0:1, :], onescol_bf, gu[:, hc, :],
                    start=False, stop=(hc == HC - 1)),
                    waits=[("sv", sv_b + hc + 1)],
                    incs=([("st", 1)] if hc == HC - 1 else []))
            S.emit("vector", lambda e: e.tensor_copy(gus_row, P[6][0:1, :]), waits=S.w("st"),
                   incs=[("sv", 1)])

            gus_ready = S.w("sv")
            dfree = []
            dn_sc = []
            ss_b = S.ctr["ss"]
            for oc in range(FC):
                if oc == 0:
                    dep, csc = dw0_dep, dw0_csc
                else:
                    if oc == 1:
                        fw = [("st", gfree[-1])]
                    else:
                        fw = [("st", dfree[oc - 2])]
                        if t == 0:
                            fw.append(("sc", dn_sc[oc - 2]))
                    dep, csc = wload(dw[oc % 2], d_t[oc], dcache[oc], fw)
                dn_sc.append(csc)
                pd = P[0] if oc % 2 == 0 else P[1]
                first_w = [dep]
                if oc == 0:
                    first_w += gus_ready
                if oc >= 2:
                    first_w += [("ss", ss_b + oc - 1)]
                for c in range(HC):
                    S.emit("tensor", lambda e, c=c, oc=oc, pd=pd: e.matmul(
                        pd[:, :], dw[oc % 2][:, c, :], gu[:, c, :],
                        start=(c == 0), stop=False),
                        waits=(first_w if c == 0 else []))
                S.emit("tensor", lambda e, pd=pd: e.matmul(
                    pd[:, :], negc_row, gus_row, start=False, stop=True),
                    incs=[("st", 1)])
                dfree.append(S.ctr["st"])
                S.emit("scalar", lambda e, oc=oc, pd=pd: e.activation(
                    aiy[:, oc, :], pd[:, :], AF.Identity, scale=sd_s[:, oc:oc + 1]),
                    waits=S.w("st"), incs=[("ss", 1)])

            S.emit("tensor", lambda e: e.nop(), waits=S.w("ss"))
            ln_stats(aiy, False)
            resid2_sv0 = S.ctr["sv"]
            ln_residual(aiy)
            for c in range(C):
                S.emit("sync", lambda e, c=c, t0=t0: e.dma_start(
                    out=out3[:, c, t0:t0 + TB], in_=XB[:, c, :]),
                    waits=[("sv", resid2_sv0 + c + 1)], incs=[("so", 16)])

        S.block()
        S.emit("sync", lambda e: e.nop(), waits=[("so", 256 * NTILES)])

        S.realize(nc, sems)
    return nc


def fbg(fb, g):
    return slice(g * 128, (g + 1) * 128)


_NC_CACHE = None


def _get_program():
    global _NC_CACHE
    if _NC_CACHE is None:
        _NC_CACHE = build_program()
    return _NC_CACHE


def kernel(L0_attn_hidden_states, L0_attn_Wq, L0_attn_Wk, L0_attn_Wv, L0_attn_Wo,
           L0_attn_bo, L0_ffn_wq_gate, L0_ffn_s_gate, L0_ffn_wq_up, L0_ffn_s_up,
           L0_ffn_wq_down, L0_ffn_s_down):
    x = np.asarray(L0_attn_hidden_states)
    S_FULL = x.shape[1]
    xT = np.ascontiguousarray(x[0].T)

    def tile_proj(W):
        return np.ascontiguousarray(
            np.asarray(W, dtype=np.float32).reshape(C, 128, FC, 128).transpose(2, 1, 0, 3))

    wq_t = tile_proj(L0_attn_Wq)
    wo_t = tile_proj(L0_attn_Wo)
    wk = np.ascontiguousarray(np.asarray(L0_attn_Wk, dtype=np.float32))
    wv = np.ascontiguousarray(np.asarray(L0_attn_Wv, dtype=np.float32))

    def tile_quant(Wq_int, out_chunks):
        W = np.asarray(Wq_int, dtype=np.int32)
        O, I = W.shape
        WT = np.ascontiguousarray(W.T)
        return np.ascontiguousarray(
            WT.reshape(I // 128, 128, out_chunks, 128).transpose(2, 1, 0, 3))

    g_t = tile_quant(L0_ffn_wq_gate, HC)
    u_t = tile_quant(L0_ffn_wq_up, HC)
    d_t = tile_quant(L0_ffn_wq_down, FC)

    def col2(v, nch):
        return np.ascontiguousarray(np.asarray(v, dtype=np.float32).reshape(nch, 128).T)

    bo2 = col2(L0_attn_bo, C)
    sg2 = col2(L0_ffn_s_gate, HC)
    su2 = col2(L0_ffn_s_up, HC)
    sd2 = col2(L0_ffn_s_down, C)

    sel32 = np.zeros((32, NHP * 128), dtype=np.float32)
    for hp in range(NHP):
        sel32[2 * hp, hp * 128:hp * 128 + 64] = 1.0
        sel32[2 * hp + 1, hp * 128 + 64:(hp + 1) * 128] = 1.0

    nc = _get_program()
    shared = {"wq_t": wq_t, "wo_t": wo_t, "wk": wk, "wv": wv, "g_t": g_t,
              "u_t": u_t, "d_t": d_t, "bo2": bo2, "sg2": sg2, "su2": su2,
              "sd2": sd2, "sel32": sel32}
    in_maps = []
    for c in range(NCORES):
        m = dict(shared)
        m["xt"] = np.ascontiguousarray(xT[:, c * TOK:(c + 1) * TOK])
        in_maps.append(m)

    res = run_bass_kernel_spmd(nc, in_maps, core_ids=list(range(NCORES)))
    out = np.empty((D, S_FULL), dtype=np.float32)
    for c in range(NCORES):
        out[:, c * TOK:(c + 1) * TOK] = res.results[c]["out_t"]
    return np.ascontiguousarray(out.T).reshape(1, S_FULL, D)


if __name__ == "__main__":
    print("building program...")
    nc = build_program()
    n = sum(1 for _ in nc.all_instructions())
    print("instructions:", n)
